# version 13
# baseline (speedup 1.0000x reference)
import numpy as np
import ml_dtypes
import concourse.bass as bass
import concourse.mybir as mybir
from concourse.bass_utils import run_bass_kernel_spmd

F32 = mybir.dt.float32
BF16 = mybir.dt.bfloat16
AF = mybir.ActivationFunctionType
ALU = mybir.AluOpType
AX = mybir.AxisListType

MAGIC = float(np.float32(1.5 * 2 ** 23))
LN2 = float(np.log(2.0))
NCORES = 8
SEQ = 2048
NSEQ = 2
D = 1024
NT = SEQ // 128
EPS_LN = 1e-5

ENGINES = ("sync", "tensor", "vector", "scalar", "gpsimd")
DMAV = ("dma0", "dma1", "dma2", "dma3")
TRACKS = ENGINES + DMAV


class Buf:

    def __init__(self, handle, name, lo, hi):
        self.h = handle
        self.name = name
        self.lo = lo
        self.hi = hi

    def __getitem__(self, key):
        return self.h[key]

    def rng(self, felo=None, fehi=None):
        if felo is None:
            return ("sb", self.lo, self.hi)
        return ("sb", self.lo + felo, min(self.hi, self.lo + fehi))


class Arena:

    def __init__(self, nc):
        self.nc = nc
        self.base = (nc.sbuf_base + 63) // 64 * 64
        self.limit = nc.sbuf_top - self.base
        self.off = 0
        self.marks = []
        self.n = 0

    def alloc(self, name, shape, dtype, align=32):
        bpe = 2 if dtype == BF16 else 4
        nbytes = int(np.prod(shape[1:])) * bpe
        off = (self.off + align - 1) // align * align
        assert off + nbytes <= self.limit, (
            f"SBUF overflow: {name} at {off}+{nbytes} > {self.limit}")
        h = self.nc.alloc_sbuf_tensor_at(
            f"{name}_{self.n}", shape, dtype, offset=self.base + off)
        self.n += 1
        self.off = off + nbytes
        return Buf(h, name, off, off + nbytes)

    def push(self):
        self.marks.append(self.off)

    def pop(self):
        self.off = self.marks.pop()

    def shared(self, name, shape, dtype):
        if not hasattr(self, "_shared"):
            self._shared = {}
        key = name
        if key in self._shared:
            buf, shp, dt = self._shared[key]
            assert tuple(shp) == tuple(shape) and dt == dtype,                 f"shared buf {name} shape/dtype mismatch"
            return buf
        buf = self.alloc(name, shape, dtype)
        self._shared[key] = (buf, tuple(shape), dtype)
        return buf


class B:

    def __init__(self, nc):
        self.nc = nc
        self.sem = {e: nc.alloc_semaphore(f"sem_{e}") for e in TRACKS}
        self.cnt = {e: 0 for e in TRACKS}
        self.ops = {e: [] for e in ENGINES}
        self.last_wait = {a: {b: 0 for b in TRACKS} for a in TRACKS}
        self.writes = []
        self.reads = []
        self.nemit = 0
        self.drain_after = {e: set() for e in ENGINES}
        self.dmarr = 0

    def _overlap(self, recs, space, lo, hi):
        out = {}
        for (s, l, h, e, c) in recs:
            if s == space and l < hi and h > lo:
                if c > out.get(e, -1):
                    out[e] = c
        return out

    SERIAL = False

    def emit(self, eng, fn, reads=(), writes=(), inc=1):
        queue = eng
        if inc == 16:
            eng = DMAV[self.dmarr % 4]
            self.dmarr += 1
        need = {}
        if B.SERIAL:
            for e in TRACKS:
                if self.cnt[e] > 0:
                    need[e] = self.cnt[e]
        for (sp, lo, hi) in reads:
            for e, c in self._overlap(self.writes, sp, lo, hi).items():
                if c > need.get(e, 0):
                    need[e] = c
        for (sp, lo, hi) in writes:
            for e, c in self._overlap(self.writes, sp, lo, hi).items():
                if c > need.get(e, 0):
                    need[e] = c
            for e, c in self._overlap(self.reads, sp, lo, hi).items():
                if e == eng and queue != "sync":
                    continue
                if c > need.get(e, 0):
                    need[e] = c
        if inc == 16:
            if self.cnt[eng] > need.get(eng, 0):
                need[eng] = self.cnt[eng]
        waits = []
        for e, c in need.items():
            if e == eng and queue != "sync" and eng == "tensor":
                continue
            if c > self.last_wait[eng][e]:
                waits.append((e, c))
                self.last_wait[eng][e] = c
        after = self.cnt[eng] + inc
        self.ops[queue].append((waits, fn, inc, eng))
        self.cnt[eng] = after
        for (sp, lo, hi) in writes:
            self.writes.append((sp, lo, hi, eng, after))
        for (sp, lo, hi) in reads:
            self.reads.append((sp, lo, hi, eng, after))
        self.nemit += 1
        if self.nemit % 4096 == 0:
            self._gc()

    def _gc(self):
        floor = {e: min(min(self.last_wait[a][e] for a in TRACKS if a != e),
                        self.cnt[e] if (e in ("sync", "tensor") or e in DMAV)
                        else (self.cnt[e] // 64) * 64)
                 for e in TRACKS}

        def keep(rec):
            return rec[4] > floor[rec[3]]
        self.writes = [r for r in self.writes if keep(r)]
        self.reads = [r for r in self.reads if keep(r)]

    def barrier(self):
        snapshot = dict(self.cnt)
        for eng in ENGINES:
            waits = []
            for e in TRACKS:
                if e != eng and snapshot[e] > self.last_wait[eng][e]:
                    waits.append((e, snapshot[e]))
                    self.last_wait[eng][e] = snapshot[e]
            if waits:
                self.ops[eng].append((waits, None, 0, eng))

    def finish(self, eng="sync"):
        waits = []
        for e in TRACKS:
            if self.cnt[e] > self.last_wait[eng][e]:
                waits.append((e, self.cnt[e]))
                self.last_wait[eng][e] = self.cnt[e]
        self.ops[eng].append((waits, None, 0, eng))

    def build(self):
        awaited = {e: set() for e in TRACKS}
        for eng in ENGINES:
            for (waits, fn, inc, trk) in self.ops[eng]:
                for (we, wc) in waits:
                    awaited[we].add(wc)
        dscr = getattr(self, "dscr", None)

        def dummy_fn(eng):
            if dscr is None:
                return None
            if eng in ("vector", "gpsimd"):
                return lambda e: e.memset(dscr[:, 0:1], 0.0)
            if eng == "scalar":
                return lambda e: e.activation(dscr[:, 1:2], dscr[:, 0:1],
                                              AF.Identity, bias=0.0,
                                              scale=0.0)
            return None

        with self.nc.Block() as blk:
            for eng in ENGINES:
                lst = self.ops[eng]
                if not lst:
                    continue

                def mk(lst, eng):
                    aw = awaited[eng]
                    dummy = dummy_fn(eng)

                    def body(e):
                        pend = 0
                        need_pub = False
                        cnt = 0
                        for waits, fn, inc, trk in lst:
                            if need_pub and (waits or fn is None
                                             or inc == 16
                                             or eng == "tensor"):
                                if dummy is not None and eng != "tensor":
                                    dummy(e).then_inc(self.sem[eng], pend)
                                else:
                                    e.drain(fusable=True).then_inc(
                                        self.sem[eng], pend)
                                pend = 0
                                need_pub = False
                            for (we, wc) in waits:
                                e.wait_ge(self.sem[we], wc)
                            if fn is None:
                                continue
                            n0 = self.nc.n_instructions()
                            ins = fn(e)
                            if inc == 16:
                                assert self.nc.n_instructions() - n0 == 1, \
                                    "dma_start split into multiple instrs"
                                ins.then_inc(self.sem[trk], 16)
                                continue
                            if eng == "tensor":
                                cnt += inc
                                pend += inc
                                if cnt in aw:
                                    ins.then_inc(self.sem[eng], pend)
                                    pend = 0
                                continue
                            if need_pub:
                                ins.then_inc(self.sem[eng], pend)
                                pend = 0
                                need_pub = False
                            cnt += inc
                            pend += inc
                            if cnt in aw:
                                need_pub = True
                            if cnt % 64 == 0:
                                d = e.drain(fusable=True)
                                if need_pub:
                                    d.then_inc(self.sem[eng], pend)
                                    pend = 0
                                    need_pub = False
                        if pend:
                            e.drain(fusable=True).then_inc(self.sem[eng],
                                                           pend)
                    return body
                getattr(blk, eng)(mk(lst, eng))


import heapq


class Sched:

    COST = {"tensor": (90, 1.0 / 9.6), "vector": (400, 0.26),
            "scalar": (300, 0.21), "gpsimd": (400, 0.5),
            "sync": (1800, 0.32)}
    DEP_LAT = 150.0

    def __init__(self, nc):
        self.nc = nc
        self.rec = []
        self.writes = {}
        self.reads = {}
        self.cnt = {e: 0 for e in TRACKS}

    def emit(self, queue, fn, reads=(), writes=(), inc=1):
        idx = len(self.rec)
        deps = set()
        for (sp, lo, hi) in reads:
            for (l, h, j) in self.writes.get(sp, ()):
                if l < hi and h > lo:
                    deps.add(j)
        for (sp, lo, hi) in writes:
            for recs in (self.writes, self.reads):
                for (l, h, j) in recs.get(sp, ()):
                    if l < hi and h > lo:
                        deps.add(j)
        for (sp, lo, hi) in writes:
            wl = self.writes.setdefault(sp, [])
            wl[:] = [(l, h, j) for (l, h, j) in wl
                     if not (lo <= l and h <= hi)]
            wl.append((lo, hi, idx))
            rl = self.reads.setdefault(sp, [])
            rl[:] = [(l, h, j) for (l, h, j) in rl
                     if not (lo <= l and h <= hi)]
        for (sp, lo, hi) in reads:
            self.reads.setdefault(sp, []).append((lo, hi, idx))
        self.rec.append((queue, fn, inc, tuple(reads), tuple(writes), deps))

    def barrier(self):
        idx = len(self.rec)
        deps = set(range(idx))
        self.rec.append(("__barrier__", None, 0, (), (), deps))

    def finish(self, eng="sync"):
        self.barrier()

    def _schedule(self):
        n = len(self.rec)
        children = [[] for _ in range(n)]
        ndep = [0] * n
        barrier_pending = []
        for i, (q, fn, inc, r, w, deps) in enumerate(self.rec):
            ndep[i] = len(deps)
            for j in deps:
                children[j].append(i)
        qavail = {}
        finish_t = [0.0] * n
        ready_t = [0.0] * n
        heap = []
        for i in range(n):
            if ndep[i] == 0:
                heapq.heappush(heap, (0.0, i))
        order = []
        pending = [0.0] * n
        while heap:
            _, i = heapq.heappop(heap)
            (q, fn, inc, r, w, deps) = self.rec[i]
            if q == "__barrier__":
                order.append(i)
                st = max([pending[i]] + list(qavail.values()) + [0.0])
                finish_t[i] = st
            else:
                fixed, perb = self.COST[q]
                span = sum(hi - lo for (sp, lo, hi) in w) or 64
                st = max(qavail.get(q, 0.0), pending[i])
                finish_t[i] = st + fixed + perb * min(span, 16384)
                qavail[q] = finish_t[i]
                order.append(i)
            for ch in children[i]:
                pending[ch] = max(pending[ch], finish_t[i] + self.DEP_LAT)
                ndep[ch] -= 1
                if ndep[ch] == 0:
                    heapq.heappush(heap, (pending[ch], ch))
        assert len(order) == n, "cycle in dep graph?"
        return order

    def build(self):
        order = self._schedule()
        b = B(self.nc)
        b.dscr = getattr(self, "dscr", None)
        for i in order:
            (q, fn, inc, r, w, deps) = self.rec[i]
            if q == "__barrier__":
                b.barrier()
            else:
                b.emit(q, fn, reads=list(r), writes=list(w), inc=inc)
        b.finish()
        b.build()



def ln_stats(c, scr, x_ap, x_rng, m_col=None, rstd_col=None, scol_rng=None,
             slot=0):
    b = c.b
    s = scr
    k = 16 * (slot % 4)
    rng = ("sb", scr.lo + 64 * (slot % 4), scr.lo + 64 * (slot % 4 + 1))
    eps = c.cbias(EPS_LN)
    b.emit("vector", lambda e, x=x_ap, o=s[:, k:k + 6]:
           e.bn_stats(o, x[:, 0:512]),
           reads=[x_rng], writes=[rng])
    b.emit("vector", lambda e, x=x_ap, o=s[:, k + 6:k + 12]:
           e.bn_stats(o, x[:, 512:1024]),
           reads=[x_rng], writes=[rng])
    b.emit("vector", lambda e, i=s[:, k:k + 12], o=s[:, k + 12:k + 14]:
           e.bn_aggr(o, i.rearrange("p (a b) -> p a b", b=6)),
           reads=[rng], writes=[rng])
    b.emit("scalar", lambda e, i=s[:, k + 13:k + 14], o=s[:, k + 14:k + 15]:
           e.activation(o, i, AF.Ln, bias=eps, scale=1.0),
           reads=[rng, c.constb.rng()], writes=[rng])
    b.emit("scalar", lambda e, i=s[:, k + 14:k + 15], o=s[:, k + 15:k + 16]:
           e.activation(o, i, AF.Exp, bias=0.0, scale=-0.5),
           reads=[rng], writes=[rng])
    return s[:, k + 12:k + 13], s[:, k + 15:k + 16], rng


def stagger(n, *stages):
    ns = len(stages)
    for k in range(n + ns - 1):
        for sidx in range(ns - 1, -1, -1):
            i = k - sidx
            if 0 <= i < n:
                stages[sidx](i)


def make_ident_and_masks(b, nc, ar, d_const):
    stg = ar.shared("stg", [128, 512], F32)
    cst = Buf(stg.h, "stg", stg.lo, stg.lo + 1024)
    b.emit("sync", lambda e: e.dma_start(cst[0:128, 0:256], d_const[:]),
           reads=[("dr:cconst", 0, 1)], writes=[cst.rng()], inc=16)
    identb = ar.alloc("identb", [128, 128], BF16)
    maskb = ar.alloc("maskb", [128, 128], BF16)
    onesb = ar.alloc("onesb", [128, 128], BF16)
    onesf = ar.alloc("onesf", [128, 128], F32)
    b.emit("vector", lambda e: e.tensor_copy(identb[:], cst.h[:, 0:128]),
           reads=[cst.rng()], writes=[identb.rng()])
    b.emit("vector", lambda e: e.tensor_copy(maskb[:], cst.h[:, 128:256]),
           reads=[cst.rng()], writes=[maskb.rng()])
    b.emit("vector", lambda e: e.memset(onesb[:], 1.0), writes=[onesb.rng()])
    b.emit("vector", lambda e: e.memset(onesf[:], 1.0), writes=[onesf.rng()])
    return cst, identb, maskb, onesb, onesf


def vec_to_cols(c, d_vec, tag, out, rowscr, psum, n=None, col0=0):
    b = c.b
    if n is None:
        n = out.h.shape[1] * 128
    k = n // 128
    pX, pXrng = psum
    for j0 in range(0, n, 512):
        w = min(512, n - j0)
        b.emit("sync", lambda e, j0=j0, w=w: e.dma_start(
            rowscr[0:1, 0:w],
            d_vec[:].rearrange("(a n) -> a n", a=1)[:, j0:j0 + w]),
            reads=[("dr:" + tag, 0, 1)], writes=[rowscr.rng()], inc=16)
        for j in range(w // 128):
            b.emit("tensor", lambda e, j=j, j0=j0: e.matmul(
                pX[:, j0 // 128 + j:j0 // 128 + j + 1],
                rowscr[0:1, 128 * j:128 * (j + 1)],
                c.onesf[0:1, 0:1], start=True, stop=True),
                reads=[rowscr.rng(), c.onesf.rng()], writes=[pXrng])
    b.emit("vector", lambda e: e.tensor_copy(out.h[:, col0:col0 + k],
                                             pX[:, 0:k]),
           reads=[pXrng], writes=[out.rng()])


def load_weight_bf16(b, ar, d_w, name, rows, cols, tag):
    k = rows // 128
    wf = ar.alloc(name + "_f", [128, k, cols], F32)
    b.emit("sync", lambda e: e.dma_start(
        wf[:], d_w[:].rearrange("(k p) n -> p k n", p=128)),
        reads=[("dr:" + tag, 0, 1)], writes=[wf.rng()], inc=16)
    wb = ar.alloc(name + "_b", [128, k, cols], BF16)
    tot = k * cols
    step = 8192
    for off in range(0, tot, step):
        n = min(step, tot - off)
        b.emit("vector", lambda e, off=off, n=n: e.tensor_copy(
            wb.h[:].rearrange("p a b -> p (a b)")[:, off:off + n],
            wf.h[:].rearrange("p a b -> p (a b)")[:, off:off + n]),
            reads=[wf.rng()], writes=[wb.rng()])
    return wf, wb


def bias_cols(b, ar, d_bias, name, n, tag):
    k = n // 128
    bb = ar.alloc(name, [128, k], F32)
    b.emit("sync", lambda e: e.dma_start(
        bb[:], d_bias[:].rearrange("(k p) -> p k", p=128)),
        reads=[("dr:" + tag, 0, 1)], writes=[bb.rng()], inc=16)
    return bb


def bias_bcast(b, ar, psum, d_bias, name, n, tag, onesf):
    row = ar.alloc(name + "_row", [1, n], F32)
    b.emit("sync", lambda e: e.dma_start(
        row[:], d_bias[:].rearrange("(a n) -> a n", a=1)),
        reads=[("dr:" + tag, 0, 1)], writes=[row.rng()], inc=16)
    out = ar.alloc(name + "_bc", [128, n], F32)
    pa, prng = psum.get(0)
    for j in range(0, n, 512):
        w = min(512, n - j)
        b.emit("tensor", lambda e, j=j, w=w: e.matmul(
            pa[:, 0:w], onesf[0:1, 0:128], row[0:1, j:j + w],
            start=True, stop=True),
            reads=[row.rng(), onesf.rng()], writes=[prng])
        b.emit("vector", lambda e, j=j, w=w: e.tensor_copy(
            out[:, j:j + w], pa[:, 0:w]),
            reads=[prng], writes=[out.rng()])
    return out


class PsumPool:

    def __init__(self, nc):
        self.t = [nc.alloc_psum_tensor(f"pp{i}", [128, 1024], F32)
                  for i in range(4)]

    def get(self, i):
        return self.t[i], ("ps", i * 4096, (i + 1) * 4096)

    def half(self, i, h):
        return (self.t[i][:, h * 512:(h + 1) * 512],
                ("ps", i * 4096 + h * 2048, i * 4096 + (h + 1) * 2048))

    def bf16(self, i):
        return (self.t[i][:].bitcast(BF16),
                ("ps", i * 4096, (i + 1) * 4096))



def regen_xn_strip(c, stream, s, st0, ntile, mstats, xn, xnstrip, identb,
                   pbank, slot0=0):
    b, pool = c.b, c.pool
    pT, pTrng = pbank
    base = pTrng[1]
    pbf = pT[:].bitcast(BF16)

    def st0f(i):
        t = st0 + i
        par = t % 2
        prng = ("ps", base + 2048 * par, base + 2048 * (par + 1))
        srng = stream.rng((NT * s + t) * 4096, (NT * s + t + 1) * 4096)
        mc = mstats[:, 2 * t:2 * t + 1]
        rc = mstats[:, 2 * t + 1:2 * t + 2]
        b.emit("vector", lambda e, o=xn[:], x=stream[:, NT * s + t, :],
               mc=mc, rc=rc:
               e.tensor_scalar(o, x, mc, rc, ALU.subtract, ALU.mult),
               reads=[srng, mstats.rng()], writes=[xn.rng()])
        for cc in range(8):
            b.emit("tensor", lambda e, cc=cc, par=par:
                   e.transpose(pbf[:, 1024 * par + 128 * cc:
                                   1024 * par + 128 * (cc + 1)],
                               xn[:, 128 * cc:128 * (cc + 1)], identb[:]),
                   reads=[xn.rng(), identb.rng()], writes=[prng])

    def st1f(i):
        t = st0 + i
        par = t % 2
        prng = ("ps", base + 2048 * par, base + 2048 * (par + 1))
        i2 = i + slot0
        _, xlo, _ = xnstrip.rng()
        wr = [("sb", xlo + 512 * cc + 256 * i2, xlo + 512 * cc + 256 * (i2 + 1))
              for cc in range(8)]
        b.emit("vector", lambda e, i=i2, par=par:
               e.tensor_copy(
                   xnstrip.h[:, :, 128 * i:128 * (i + 1)],
                   pbf[:, 1024 * par:1024 * (par + 1)]
                   .rearrange("p (a b) -> p a b", b=128)),
               reads=[prng], writes=wr)
    stagger(ntile, st0f, st1f)



WEIGHT_SPECS = {
    "l0attn": [("L0_attn_Wq", (D, D)), ("L0_attn_bq", (D,)),
               ("L0_attn_Wk", (D, D)), ("L0_attn_bk", (D,)),
               ("L0_attn_Wv", (D, D)), ("L0_attn_bv", (D,))],
    "l0ffn": [("L0_ffn_W_r", (D, D)), ("L0_ffn_W_k", (D, D)),
              ("L0_ffn_W_v", (D, D)), ("L0_ffn_mu_r", (D,)),
              ("L0_ffn_mu_k", (D,))],
    "l1attn": [("L1_attn_w_lqkv", (512, 1536)), ("L1_attn_b_lqkv", (1536,)),
               ("L1_attn_w_gqkv", (512, 1536)), ("L1_attn_b_gqkv", (1536,)),
               ("L1_attn_w_comp", (4096, D)), ("L1_attn_b_comp", (D,)),
               ("L1_attn_w_lproj", (512, 512)), ("L1_attn_b_lproj", (512,)),
               ("L1_attn_w_gproj", (512, 512)), ("L1_attn_b_gproj", (512,)),
               ("L1_attn_w_gate", (D, 2)), ("L1_attn_b_gate", (2,))],
    "l1ffn": [("L1_ffn_weight", (D, D)), ("L1_ffn_bias", (D,))],
}

EMITTERS = {}


DEBUG_TAPS = []
TRUNC = 0
NOPOP = 0
PV_PSUM = True
CG_PSUM = False


class Ctx:
    def __init__(self, nc):
        self.nc = nc
        self.b = Sched(nc)
        self.ar = Arena(nc)
        self.pool = PsumPool(nc)
        self.constb = self.ar.alloc("constb", [128, 8], F32)
        self._cvals = {}
        self.dbg = None
        self.dbgcur = 0
        self.dbgmap = {}
        if DEBUG_TAPS:
            self.d_dbg = nc.dram_tensor("dbgout", [128, 16384], F32,
                                        kind="ExternalOutput")
            self.dbg = self.ar.alloc("dbgbuf", [128, 512], F32)

    def tap(self, tag, ap, rng, ncols):
        if self.dbg is None or tag not in DEBUG_TAPS:
            return
        assert self.dbgcur + ncols <= 16384, "debug overflow"
        cur = self.dbgcur
        self.dbgmap[tag] = (cur, ncols)
        self.dbgcur += ncols
        for off in range(0, ncols, 512):
            n = min(512, ncols - off)
            self.b.emit("vector", lambda e, ap=ap, off=off, n=n:
                        e.tensor_copy(self.dbg[:, 0:n], ap[:, off:off + n]),
                        reads=[rng], writes=[self.dbg.rng()])
            self.b.emit("sync", lambda e, cur=cur, off=off, n=n: e.dma_start(
                self.d_dbg[:, cur + off:cur + off + n], self.dbg[:, 0:n]),
                reads=[self.dbg.rng()], writes=[("dr:dbg", cur + off,
                                                 cur + off + n)], inc=16)

    def lnslot(self):
        self._lns = getattr(self, "_lns", 0) + 1
        return self._lns % 4

    def cbias(self, value):
        v = float(value)
        if v not in self._cvals:
            assert len(self._cvals) < 8
            j = len(self._cvals)
            self._cvals[v] = j
            self.b.emit("vector", lambda e, j=j, v=v: e.memset(
                self.constb[:, j:j + 1], v), writes=[self.constb.rng()])
        return self.constb[:, self._cvals[v]:self._cvals[v] + 1]


def make_consts():
    ident = np.eye(128, dtype=np.float32)
    mask = (np.arange(128)[:, None] <= np.arange(128)[None, :]
            ).astype(np.float32)
    return np.concatenate([ident, mask], axis=1)


def build_program(blocks):
    nc = bass.Bass("TRN2", target_bir_lowering=False)
    c = Ctx(nc)
    b, ar = c.b, c.ar
    d = {}
    d["xs"] = nc.dram_tensor("xs", [NSEQ, SEQ, D], F32, kind="ExternalInput")
    d["cconst"] = nc.dram_tensor("cconst", [128, 256], F32,
                                 kind="ExternalInput")
    for blk in blocks:
        for (nm, shp) in WEIGHT_SPECS[blk]:
            d[nm] = nc.dram_tensor(nm, list(shp), F32, kind="ExternalInput")
    d_y = nc.dram_tensor("ys", [NSEQ, SEQ, D], F32, kind="ExternalOutput")

    stream = ar.alloc("stream", [128, NSEQ * NT, 1024], F32)
    b.dscr = ar.shared("bdummy", [128, 8], F32).h
    c.cst, c.identb, c.maskb, c.onesb, c.onesf = make_ident_and_masks(
        b, nc, ar, d["cconst"])

    for st in range(NSEQ * NT):
        s, t = divmod(st, NT)
        b.emit("sync", lambda e, s=s, t=t, st=st: e.dma_start(
            stream.h[:, st, :], d["xs"][s, 128 * t:128 * (t + 1), :]),
            reads=[("dr:xs", 0, 1)],
            writes=[stream.rng(4096 * st, 4096 * (st + 1))], inc=16)

    for blk in blocks:
        EMITTERS[blk](c, stream, d)

    for st in range(NSEQ * NT):
        s, t = divmod(st, NT)
        b.emit("sync", lambda e, s=s, t=t, st=st: e.dma_start(
            d_y[s, 128 * t:128 * (t + 1), :], stream.h[:, st, :]),
            reads=[stream.rng(4096 * st, 4096 * (st + 1))],
            writes=[("dr:ys", st, st + 1)], inc=16)
    b.build()
    return nc


_PROG_CACHE = {}


def get_program(blocks):
    key = tuple(blocks)
    if key not in _PROG_CACHE:
        _PROG_CACHE[key] = build_program(list(blocks))
    return _PROG_CACHE[key]


FUSED = True
_BLOCK_ORDER = ["l0attn", "l0ffn", "l1attn", "l1ffn"]


def kernel(**inputs):
    consts = make_consts()
    x = np.ascontiguousarray(inputs["L0_attn_x"], dtype=np.float32)
    launches = ([_BLOCK_ORDER] if FUSED
                else [[blk] for blk in _BLOCK_ORDER])
    cur = x
    for blocks in launches:
        nc = get_program(blocks)
        in_maps = []
        for core in range(NCORES):
            m = {"xs": cur[2 * core:2 * core + 2], "cconst": consts}
            for blk in blocks:
                for (nm, _) in WEIGHT_SPECS[blk]:
                    m[nm] = inputs[nm]
            in_maps.append(m)
        res = run_bass_kernel_spmd(nc, in_maps, core_ids=list(range(NCORES)))
        cur = np.concatenate([np.asarray(r["ys"]) for r in res.results],
                             axis=0)
    return cur.reshape(16, SEQ, D)





def emit_l0attn(c, stream, d):
    b, ar, pool, nc = c.b, c.ar, c.pool, c.nc
    identb, onesb = c.identb, c.onesb
    mstats = ar.shared("mstats", [128, 32], F32)
    scol = ar.shared("scol", [128, 64], F32)
    xn = ar.shared("xn", [128, 1024], BF16)
    kvb = ar.shared("kvb", [128, 8, 128], BF16)
    bcolsx = ar.shared("bcols", [128, 32], BF16)
    ksb = Buf(bcolsx.h[:, 24:32], "bcols", bcolsx.lo + 48, bcolsx.lo + 64)
    stg = ar.shared("stg", [128, 512], F32)
    kvacc = ar.shared("facc", [128, 1024], F32)
    fout = ar.shared("fout", [128, 1024], F32)
    ksacc = ar.shared("cols0", [128, 8], F32)
    scr2 = ar.shared("fs0", [128, 512], F32)
    ao = kvacc
    zr = ar.shared("cols1", [128, 8], F32)
    rowf = Buf(stg.h, "stg", stg.lo, stg.hi)
    bkr = Buf(fout.h[0:1, 0:512].bitcast(BF16), "fout", fout.lo, fout.lo + 2048)
    bvr = Buf(fout.h[0:1, 512:1024].bitcast(BF16), "fout",
              fout.lo + 2048, fout.lo + 4096)
    bqc = ar.shared("cols2", [128, 8], F32)
    nbqc = ar.shared("cols3", [128, 8], F32)
    strip0 = ar.shared("strip0", [128, 8, 256], BF16)
    strip1 = ar.shared("strip1", [128, 8, 256], BF16)
    s1flat = strip1.h[:].rearrange("p a b -> p (a b)")
    vtm = Buf(s1flat[:, 0:1024], "strip1", strip1.lo, strip1.lo + 2048)
    ktm = Buf(s1flat[:, 1024:2048], "strip1", strip1.lo + 2048,
              strip1.lo + 4096)
    wb0 = ar.shared("wb0", [128, 8, 1024], BF16)
    wb1 = ar.shared("wb1", [128, 8, 1024], BF16)

    def load_w(nm, wb):
        for kc in range(8):
            for hf in range(2):
                b.emit("sync", lambda e, nm=nm, kc=kc, hf=hf: e.dma_start(
                    stg[:], d[nm][128 * kc:128 * (kc + 1),
                                  512 * hf:512 * (hf + 1)]),
                    reads=[("dr:" + nm, 0, 1)], writes=[stg.rng()], inc=16)
                b.emit("vector", lambda e, wb=wb, kc=kc, hf=hf: e.tensor_copy(
                    wb.h[:, kc, 512 * hf:512 * (hf + 1)], stg[:]),
                    reads=[stg.rng()], writes=[wb.rng()])

    for (nm, buf) in (("L0_attn_bk", bkr), ("L0_attn_bv", bvr)):
        for hf in range(2):
            b.emit("sync", lambda e, nm=nm, hf=hf: e.dma_start(
                rowf[0:1, 0:512], d[nm][:].rearrange("(a n) -> a n", a=1)
                [:, 512 * hf:512 * (hf + 1)]),
                reads=[("dr:" + nm, 0, 1)], writes=[rowf.rng()], inc=16)
            b.emit("vector", lambda e, buf=buf, hf=hf: e.tensor_copy(
                buf.h[:, 512 * hf:512 * (hf + 1)], rowf[0:1, 0:512]),
                reads=[rowf.rng()], writes=[buf.rng()])
    vec_to_cols(c, d["L0_attn_bq"], "L0_attn_bq", bqc, rowf, c.pool.get(0))
    b.emit("vector", lambda e: e.tensor_scalar_mul(nbqc[:], bqc[:], -1.0),
           reads=[bqc.rng()], writes=[nbqc.rng()])

    for s in range(NSEQ):
        for t in range(NT):
            srng = stream.rng((NT * s + t) * 4096, (NT * s + t + 1) * 4096)
            m, r, lrng = ln_stats(c, scol, stream[:, NT * s + t, :], srng,
                                  slot=c.lnslot())
            b.emit("vector", lambda e, m=m, t=t: e.tensor_copy(
                mstats[:, 2 * t:2 * t + 1], m),
                reads=[lrng], writes=[mstats.rng()])
            b.emit("vector", lambda e, r=r, t=t: e.tensor_copy(
                mstats[:, 2 * t + 1:2 * t + 2], r),
                reads=[lrng], writes=[mstats.rng()])

        load_w("L0_attn_Wk", wb0)
        load_w("L0_attn_Wv", wb1)
        pA, pArng = pool.get(0)
        pB, pBrng = pool.get(1)
        pC, pCrng = pool.get(2)
        pD, pDrng = pool.get(3)
        b.emit("vector", lambda e: e.memset(kvacc[:], 0.0),
               writes=[kvacc.rng()])
        b.emit("vector", lambda e: e.memset(ksacc[:], 0.0),
               writes=[ksacc.rng()])

        def p1_regen(t):
            regen_xn_strip(c, stream, s, t, 1, mstats, xn, strip0, identb,
                           pool.get(2), slot0=t % 2)

        def p1_mm(t):
            q = t % 2
            for (wb, brow, pX, pXrng2) in ((wb0, bkr, pA, pArng),
                                           (wb1, bvr, pB, pBrng)):
                for h in range(2):
                    for cc in range(8):
                        b.emit("tensor",
                               lambda e, wb=wb, h=h, cc=cc, pX=pX, q=q:
                               e.matmul(pX[:, 512 * h:512 * (h + 1)],
                                        strip0.h[:, cc,
                                                 128 * q:128 * (q + 1)],
                                        wb.h[:, cc, 512 * h:512 * (h + 1)],
                                        start=(cc == 0), stop=False),
                               reads=[strip0.rng(), wb.rng()],
                               writes=[pXrng2])
                    b.emit("tensor", lambda e, brow=brow, h=h, pX=pX:
                           e.matmul(pX[:, 512 * h:512 * (h + 1)],
                                    onesb[0:1, 0:128],
                                    brow[0:1, 512 * h:512 * (h + 1)],
                                    start=False, stop=True),
                           reads=[brow.rng(), onesb.rng()], writes=[pXrng2])

        def p1_tail(t):
            b.emit("vector", lambda e: e.tensor_copy(vtm.h, pB[:]),
                   reads=[pBrng], writes=[vtm.rng()])
            for q4 in range(4):
                sl = slice(256 * q4, 256 * (q4 + 1))
                b.emit("scalar", lambda e, sl=sl: e.activation(
                    scr2[:, 0:256], pA[:, sl], AF.Relu, bias=0.0,
                    scale=-1.0),
                    reads=[pArng], writes=[scr2.rng()])
                b.emit("scalar", lambda e: e.activation(
                    scr2[:, 256:512], scr2[:, 0:256], AF.Exp, bias=0.0,
                    scale=-1.0),
                    reads=[scr2.rng()], writes=[scr2.rng()])
                b.emit("vector", lambda e, sl=sl: e.tensor_scalar_max(
                    scr2[:, 0:256], pA[:, sl], 0.0),
                    reads=[pArng], writes=[scr2.rng()])
                b.emit("vector", lambda e, sl=sl: e.tensor_tensor(
                    ktm.h[:, sl], scr2[:, 256:512], scr2[:, 0:256],
                    ALU.add),
                    reads=[scr2.rng()], writes=[ktm.rng()])
            for hh in range(8):
                b.emit("tensor", lambda e, hh=hh: e.matmul(
                    pD[:, 128 * hh:128 * (hh + 1)],
                    ktm.h[:, 128 * hh:128 * (hh + 1)],
                    vtm.h[:, 128 * hh:128 * (hh + 1)],
                    start=True, stop=True),
                    reads=[ktm.rng(), vtm.rng()], writes=[pDrng])
                b.emit("tensor", lambda e, hh=hh: e.matmul(
                    pC[:, 512 + hh:512 + hh + 1],
                    ktm.h[:, 128 * hh:128 * (hh + 1)],
                    onesb[:, 0:1],
                    start=True, stop=True),
                    reads=[ktm.rng(), onesb.rng()],
                    writes=[("ps", 2 * 4096 + 2048, 3 * 4096)])
            b.emit("vector", lambda e: e.tensor_tensor(
                kvacc[:], kvacc[:], pD[:], ALU.add),
                reads=[kvacc.rng(), pDrng], writes=[kvacc.rng()])
            b.emit("vector", lambda e: e.tensor_tensor(
                ksacc[:], ksacc[:], pC[:, 512:520], ALU.add),
                reads=[ksacc.rng(), ("ps", 2 * 4096 + 2048, 3 * 4096)],
                writes=[ksacc.rng()])
        stagger(NT, p1_regen, p1_mm, p1_tail)
        b.emit("vector", lambda e: e.tensor_copy(
            kvb.h[:].rearrange("p a b -> p (a b)"), kvacc[:]),
            reads=[kvacc.rng()], writes=[kvb.rng()])
        b.emit("vector", lambda e: e.tensor_copy(ksb.h, ksacc[:]),
               reads=[ksacc.rng()], writes=[ksb.rng()])

        load_w("L0_attn_Wq", wb0)
        halves = [pool.half(0, 0), pool.half(0, 1),
                  pool.half(1, 0), pool.half(1, 1)]
        wb2x = ar.shared("wb2", [128, 10368], BF16)
        stA = [strip0,
               Buf(wb2x.h[:, 0:2048].rearrange("p (a b) -> p a b", b=256),
                   "wb2", wb2x.lo, wb2x.lo + 4096)]
        stQ = [strip1,
               Buf(wb2x.h[:, 2048:4096].rearrange("p (a b) -> p a b", b=256),
                   "wb2", wb2x.lo + 4096, wb2x.lo + 8192)]

        def p2_regen(st):
            regen_xn_strip(c, stream, s, 2 * st, 2, mstats, xn,
                           stA[st % 2], identb, pool.get(2))

        def p2_qproj(st):
            sa, sq = stA[st % 2], stQ[st % 2]
            for o in range(8):
                ph, phrng = halves[o % 4]
                for cc in range(8):
                    b.emit("tensor",
                           lambda e, o=o, cc=cc, ph=ph, sa=sa: e.matmul(
                               ph[:, 0:256],
                               wb0.h[:, cc, 128 * o:128 * (o + 1)],
                               sa.h[:, cc, :],
                               start=(cc == 0), stop=(cc == 7)),
                           reads=[sa.rng(), wb0.rng()], writes=[phrng])
                b.emit("scalar", lambda e, o=o, ph=ph: e.activation(
                    scr2[:, 0:256], ph[:, 0:256], AF.Relu,
                    bias=nbqc[:, o:o + 1], scale=-1.0),
                    reads=[phrng, nbqc.rng()], writes=[scr2.rng()])
                b.emit("scalar", lambda e: e.activation(
                    scr2[:, 256:512], scr2[:, 0:256], AF.Exp,
                    bias=0.0, scale=-1.0),
                    reads=[scr2.rng()], writes=[scr2.rng()])
                b.emit("scalar", lambda e, o=o, ph=ph: e.activation(
                    scr2[:, 0:256], ph[:, 0:256], AF.Relu,
                    bias=bqc[:, o:o + 1], scale=1.0),
                    reads=[phrng, bqc.rng()], writes=[scr2.rng()])
                b.emit("vector", lambda e, o=o, sq=sq: e.tensor_tensor(
                    sq.h[:, o, :], scr2[:, 256:512], scr2[:, 0:256],
                    ALU.add),
                    reads=[scr2.rng()], writes=[sq.rng()])

        def p2_out(st):
            sq = stQ[st % 2]
            qdrng = ("ps", 3 * 4096, 3 * 4096 + 32)
            for i in range(2):
                t = 2 * st + i
                tsl = slice(128 * i, 128 * (i + 1))
                for hh in range(8):
                    b.emit("tensor",
                           lambda e, hh=hh, tsl=tsl, sq=sq: e.matmul(
                               pD[:, hh:hh + 1], sq.h[:, hh, tsl],
                               ksb.h[:, hh:hh + 1], start=True, stop=True),
                           reads=[sq.rng(), ksb.rng()], writes=[qdrng])
                b.emit("vector", lambda e: e.tensor_scalar(
                    zr[:], pD[:, 0:8], 1e-6, None, ALU.add),
                    reads=[qdrng], writes=[zr.rng()])
                b.emit("vector", lambda e: e.reciprocal(zr[:], zr[:]),
                       reads=[zr.rng()], writes=[zr.rng()])
                for hh in range(8):
                    b.emit("tensor",
                           lambda e, hh=hh, tsl=tsl, sq=sq: e.matmul(
                               pD[:, 128 * hh:128 * (hh + 1)],
                               sq.h[:, hh, tsl],
                               kvb.h[:, hh, :], start=True, stop=True),
                           reads=[sq.rng(), kvb.rng()], writes=[pDrng])
                b.emit("vector", lambda e: e.tensor_tensor(
                    ao[:].rearrange("p (a b) -> p a b", b=128),
                    pD[:].rearrange("p (a b) -> p a b", b=128),
                    zr[:].rearrange("p (a b) -> p a b", b=1)
                    .to_broadcast([128, 8, 128]),
                    ALU.mult),
                    reads=[pDrng, zr.rng()], writes=[ao.rng()])
                srng = stream.rng((NT * s + t) * 4096,
                                  (NT * s + t + 1) * 4096)
                m, r, lrng = ln_stats(c, scol, ao[:], ao.rng(),
                                      slot=c.lnslot())
                b.emit("vector", lambda e, m=m, r=r: e.tensor_scalar(
                    ao[:], ao[:], m, r, ALU.subtract, ALU.mult),
                    reads=[ao.rng(), lrng], writes=[ao.rng()])
                b.emit("vector", lambda e, s=s, t=t: e.tensor_tensor(
                    stream[:, NT * s + t, :], stream[:, NT * s + t, :],
                    ao[:], ALU.add),
                    reads=[srng, ao.rng()], writes=[srng])
        stagger(8, p2_regen, p2_qproj, p2_out)


EMITTERS["l0attn"] = emit_l0attn



def load_w_transposed(c, d, nm, wb, stg, tmb, identb):
    b, pool = c.b, c.pool
    pD, pDrng = pool.get(3)
    pbf = pD[:].bitcast(BF16)
    for kc in range(8):
        for hf in range(2):
            b.emit("sync", lambda e, nm=nm, kc=kc, hf=hf: e.dma_start(
                stg[:], d[nm][128 * kc:128 * (kc + 1),
                              512 * hf:512 * (hf + 1)]),
                reads=[("dr:" + nm, 0, 1)], writes=[stg.rng()], inc=16)
            b.emit("vector", lambda e, hf=hf: e.tensor_copy(
                tmb.h[:, 512 * hf:512 * (hf + 1)], stg[:]),
                reads=[stg.rng()], writes=[tmb.rng()])
        for cc in range(8):
            b.emit("tensor", lambda e, cc=cc, pbf=pbf:
                   e.transpose(pbf[:, 128 * cc:128 * (cc + 1)],
                               tmb.h[:, 128 * cc:128 * (cc + 1)], identb[:]),
                   reads=[tmb.rng(), identb.rng()], writes=[pDrng])
        b.emit("vector", lambda e, kc=kc, pbf=pbf: e.tensor_copy(
            wb.h[:, :, 128 * kc:128 * (kc + 1)],
            pbf[:, 0:1024].rearrange("p (a b) -> p a b", b=128)),
            reads=[pDrng], writes=[wb.rng()])


def emit_l0ffn(c, stream, d):
    b, ar, pool, nc = c.b, c.ar, c.pool, c.nc
    identb, onesb = c.identb, c.onesb
    scol = ar.shared("scol", [128, 64], F32)
    xn = ar.shared("xn", [128, 1024], BF16)
    stg = ar.shared("stg", [128, 512], F32)
    rbuf = ar.shared("facc", [128, 1024], F32)
    fout = ar.shared("fout", [128, 1024], F32)
    fs0 = ar.shared("fs0", [128, 512], F32)
    strip0 = ar.shared("strip0", [128, 8, 256], BF16)
    strip1 = ar.shared("strip1", [128, 8, 256], BF16)
    wb0 = ar.shared("wb0", [128, 8, 1024], BF16)
    wb1 = ar.shared("wb1", [128, 8, 1024], BF16)
    wb2 = ar.shared("wb2", [128, 10368], BF16)
    kvbb = ar.shared("kvb", [128, 8, 128], BF16)
    muc = ar.shared("cols4", [128, 8], F32)
    ktmb = Buf(strip1.h[:].rearrange("p a b -> p (a b)")[:, 1024:2048],
               "strip1", strip1.lo + 2048, strip1.lo + 4096)
    katile = Buf(kvbb.h, "kvb", kvbb.lo, kvbb.hi)
    bcols = ar.shared("bcols", [128, 32], BF16)

    rowf = Buf(stg.h, "stg", stg.lo, stg.hi)
    for (j, nm) in ((0, "L0_ffn_mu_r"), (2, "L0_ffn_mu_k")):
        vec_to_cols(c, d[nm], nm, muc, rowf, pool.get(0), n=1024, col0=0)
        b.emit("vector", lambda e, j=j: e.tensor_copy(
            bcols[:, 8 * j:8 * (j + 1)], muc.h[:, 0:8]),
            reads=[muc.rng()], writes=[bcols.rng()])
        b.emit("vector", lambda e: e.tensor_scalar(
            muc.h[:, 0:8], muc.h[:, 0:8], -1.0, 1.0, ALU.mult, ALU.add),
            reads=[muc.rng()], writes=[muc.rng()])
        b.emit("vector", lambda e, j=j: e.tensor_copy(
            bcols[:, 8 * (j + 1):8 * (j + 2)], muc.h[:, 0:8]),
            reads=[muc.rng()], writes=[bcols.rng()])

    load_w_transposed(c, d, "L0_ffn_W_r", wb0, stg, ktmb, identb)
    load_w_transposed(c, d, "L0_ffn_W_k", wb1, stg, ktmb, identb)
    wb2v = Buf(wb2.h[:, 0:8192].rearrange("p (a b) -> p a b", b=1024),
               "wb2", wb2.lo, wb2.lo + 16384)
    load_w_transposed(c, d, "L0_ffn_W_v", wb2v, stg, ktmb, identb)

    def mucol(j):
        return bcols.h[:, 8 * j:8 * (j + 1)].rearrange(
            "p (a b) -> p a b", b=1)

    pA, pArng = pool.get(0)
    pB, pBrng = pool.get(1)
    pC, pCrng = pool.get(2)
    tmp = Buf(xn.h[:].rearrange("p (a b) -> p a b", b=128), "xn", xn.lo, xn.hi)

    w2tail = 8192

    def mixbuf(t):
        if t % 2 == 0:
            xr = strip1.h[:, :, 0:128]
            xk = strip1.h[:, :, 128:256]
            rng = strip1.rng()
        else:
            f = wb2.h[:]
            xr = f[:, w2tail:w2tail + 1024].rearrange(
                "p (a b) -> p a b", b=128)
            xk = f[:, w2tail + 1024:w2tail + 2048].rearrange(
                "p (a b) -> p a b", b=128)
            rng = ("sb", wb2.lo + 2 * w2tail, wb2.lo + 2 * (w2tail + 2048))
        return xr, xk, rng

    for s in range(NSEQ):
        def stageA(t):
            srng = stream.rng((NT * s + t) * 4096, (NT * s + t + 1) * 4096)
            q = t % 2
            yT = strip0.h[:, :, 128 * q:128 * (q + 1)]
            yTp = strip0.h[:, :, 128 * (1 - q):128 * (2 - q)]
            m, r, lrng = ln_stats(c, scol, stream[:, NT * s + t, :], srng,
                                  slot=c.lnslot())
            b.emit("vector", lambda e, m=m, r=r, s=s, t=t: e.tensor_scalar(
                xn[:], stream[:, NT * s + t, :], m, r,
                ALU.subtract, ALU.mult),
                reads=[srng, lrng], writes=[xn.rng()])
            pbf = pC[:].bitcast(BF16)
            prng = ("ps", pCrng[1] + 2048 * q, pCrng[1] + 2048 * (q + 1))
            for cc in range(8):
                b.emit("tensor", lambda e, cc=cc, pbf=pbf, q=q:
                       e.transpose(pbf[:, 1024 * q + 128 * cc:
                                       1024 * q + 128 * (cc + 1)],
                                   xn[:, 128 * cc:128 * (cc + 1)],
                                   identb[:]),
                       reads=[xn.rng(), identb.rng()], writes=[prng])
            b.emit("vector", lambda e, q=q, pbf=pbf: e.tensor_copy(
                strip0.h[:, :, 128 * q:128 * (q + 1)],
                pbf[:, 1024 * q:1024 * (q + 1)]
                .rearrange("p (a b) -> p a b", b=128)),
                reads=[prng], writes=[strip0.rng()])
            xrT, xkT, mrng = mixbuf(t)
            for (xv, jmu) in ((xrT, 0), (xkT, 2)):
                b.emit("vector", lambda e, xv=xv, jmu=jmu, yT=yT:
                       e.tensor_tensor(
                           xv[:, :, 1:128], yT[:, :, 1:128],
                           mucol(jmu).to_broadcast([128, 8, 127]), ALU.mult),
                       reads=[strip0.rng(), bcols.rng()], writes=[mrng])
                b.emit("vector", lambda e, jmu=jmu, yT=yT:
                       e.tensor_tensor(
                           tmp.h[:, :, 0:127], yT[:, :, 0:127],
                           mucol(jmu + 1).to_broadcast([128, 8, 127]),
                           ALU.mult),
                       reads=[strip0.rng(), bcols.rng()], writes=[tmp.rng()])
                b.emit("vector", lambda e, xv=xv:
                       e.tensor_tensor(xv[:, :, 1:128], xv[:, :, 1:128],
                                       tmp.h[:, :, 0:127], ALU.add),
                       reads=[mrng, tmp.rng()], writes=[mrng])
                b.emit("vector", lambda e, xv=xv, jmu=jmu, yT=yT:
                       e.tensor_tensor(
                           xv[:, :, 0:1], yT[:, :, 0:1],
                           mucol(jmu).to_broadcast([128, 8, 1]), ALU.mult),
                       reads=[strip0.rng(), bcols.rng()], writes=[mrng])
                if t > 0:
                    b.emit("vector", lambda e, jmu=jmu, yTp=yTp:
                           e.tensor_tensor(
                               tmp.h[:, :, 127:128], yTp[:, :, 127:128],
                               mucol(jmu + 1).to_broadcast([128, 8, 1]),
                               ALU.mult),
                           reads=[strip0.rng(), bcols.rng()],
                           writes=[tmp.rng()])
                    b.emit("vector", lambda e, xv=xv:
                           e.tensor_tensor(xv[:, :, 0:1], xv[:, :, 0:1],
                                           tmp.h[:, :, 127:128], ALU.add),
                           reads=[mrng, tmp.rng()], writes=[mrng])

        def stageB(t):
            xrT, xkT, mrng = mixbuf(t)
            for h in range(2):
                for cc in range(8):
                    b.emit("tensor", lambda e, h=h, cc=cc, xrT=xrT: e.matmul(
                        pA[:, 512 * h:512 * (h + 1)],
                        xrT[:, cc, :],
                        wb0.h[:, cc, 512 * h:512 * (h + 1)],
                        start=(cc == 0), stop=(cc == 7)),
                        reads=[mrng, wb0.rng()], writes=[pArng])
            b.emit("scalar", lambda e: e.activation(
                rbuf[:], pA[:], AF.Sigmoid, bias=0.0, scale=1.0),
                reads=[pArng], writes=[rbuf.rng()])
            for o in range(8):
                for cc in range(8):
                    b.emit("tensor", lambda e, o=o, cc=cc, xkT=xkT: e.matmul(
                        pB[:, 128 * o:128 * (o + 1)],
                        wb1.h[:, cc, 128 * o:128 * (o + 1)],
                        xkT[:, cc, :],
                        start=(cc == 0), stop=(cc == 7)),
                        reads=[mrng, wb1.rng()], writes=[pBrng])
                b.emit("scalar", lambda e, o=o: e.activation(
                    fs0[:, 0:128], pB[:, 128 * o:128 * (o + 1)], AF.Relu,
                    bias=0.0, scale=1.0),
                    reads=[pBrng], writes=[fs0.rng()])
                b.emit("scalar", lambda e, o=o: e.activation(
                    katile.h[:, o, :], fs0[:, 0:128], AF.Square,
                    bias=0.0, scale=1.0),
                    reads=[fs0.rng()], writes=[katile.rng()])

        def stageC(t):
            srng = stream.rng((NT * s + t) * 4096, (NT * s + t + 1) * 4096)
            pD2, pD2rng = pool.get(3)
            for h in range(2):
                for dc in range(8):
                    b.emit("tensor", lambda e, h=h, dc=dc, pD2=pD2: e.matmul(
                        pD2[:, 512 * h:512 * (h + 1)],
                        katile.h[:, dc, :],
                        wb2v.h[:, dc, 512 * h:512 * (h + 1)],
                        start=(dc == 0), stop=(dc == 7)),
                        reads=[katile.rng(), wb2v.rng()], writes=[pD2rng])
            b.emit("vector", lambda e, pD2=pD2: e.tensor_tensor(
                fout[:], rbuf[:], pD2[:], ALU.mult),
                reads=[rbuf.rng(), pD2rng], writes=[fout.rng()])
            m, r, lrng = ln_stats(c, scol, fout[:], fout.rng(),
                                  slot=c.lnslot())
            b.emit("vector", lambda e, m=m, r=r: e.tensor_scalar(
                fout[:], fout[:], m, r, ALU.subtract, ALU.mult),
                reads=[fout.rng(), lrng], writes=[fout.rng()])
            b.emit("vector", lambda e, s=s, t=t: e.tensor_tensor(
                stream[:, NT * s + t, :], stream[:, NT * s + t, :],
                fout[:], ALU.add),
                reads=[srng, fout.rng()], writes=[srng])
        stagger(NT, stageA, stageB, stageC)


EMITTERS["l0ffn"] = emit_l0ffn



def emit_l1attn(c, stream, d):
    b, ar, pool, nc = c.b, c.ar, c.pool, c.nc
    identb, onesb, onesf, maskb = c.identb, c.onesb, c.onesf, c.maskb
    mstats = ar.shared("mstats", [128, 32], F32)
    scol = ar.shared("scol", [128, 64], F32)
    xn = ar.shared("xn", [128, 1024], BF16)
    stg = ar.shared("stg", [128, 512], F32)
    facc = ar.shared("facc", [128, 1024], F32)
    fout = ar.shared("fout", [128, 1024], F32)
    fs0 = ar.shared("fs0", [128, 512], F32)
    kvb = ar.shared("kvb", [128, 8, 128], BF16)
    strip0 = ar.shared("strip0", [128, 8, 256], BF16)
    strip1 = ar.shared("strip1", [128, 8, 256], BF16)
    wb0 = ar.shared("wb0", [128, 8, 1024], BF16)
    wb1 = ar.shared("wb1", [128, 8, 1024], BF16)
    wb2 = ar.shared("wb2", [128, 10368], BF16)
    bcols = ar.shared("bcols", [128, 32], BF16)
    colsA = ar.shared("colsA", [128, 16], F32)
    colsB = ar.shared("colsB", [128, 16], F32)
    colsC = ar.shared("colsC", [128, 8], F32)
    wgc = ar.shared("cols4", [128, 8], F32)
    rowf = Buf(stg.h, "stg", stg.lo, stg.hi)

    s0f = strip0.h[:].rearrange("p a b -> p (a b)")
    s1f = strip1.h[:].rearrange("p a b -> p (a b)")

    def s0r(lo, hi):
        return ("sb", strip0.lo + 2 * lo, strip0.lo + 2 * hi)

    def s1r(lo, hi):
        return ("sb", strip1.lo + 2 * lo, strip1.lo + 2 * hi)

    def w0r(lo, hi):
        return ("sb", wb0.lo + 2 * lo, wb0.lo + 2 * hi)

    def w2r(lo, hi):
        return ("sb", wb2.lo + 2 * lo, wb2.lo + 2 * hi)
    w0f = wb0.h[:].rearrange("p a b -> p (a b)")
    w1f = wb1.h[:].rearrange("p a b -> p (a b)")
    w2f = wb2.h[:]
    W2RNG = wb2.rng()
    S0RNG, S1RNG = strip0.rng(), strip1.rng()
    W0RNG, W1RNG = wb0.rng(), wb1.rng()
    CGOFF = 8320

    pA, pArng = pool.get(0)
    pB, pBrng = pool.get(1)
    pC, pCrng = pool.get(2)
    pD, pDrng = pool.get(3)

    vec_to_cols(c, d["L1_attn_b_lqkv"], "b_lqkv", colsA, rowf,
                pool.get(0), n=1536, col0=0)
    vec_to_cols(c, d["L1_attn_b_comp"], "b_comp", colsA, rowf,
                pool.get(0), n=1024, col0=8)
    bcomp_c = colsA.h[:, 12:16]
    vec_to_cols(c, d["L1_attn_b_gqkv"], "b_gqkv", colsB, rowf,
                pool.get(0), n=1536, col0=0)
    vec_to_cols(c, d["L1_attn_b_lproj"], "b_lproj", colsB, rowf,
                pool.get(0), n=512, col0=12)
    vec_to_cols(c, d["L1_attn_b_gproj"], "b_gproj", colsC, rowf,
                pool.get(0), n=512, col0=0)
    b.emit("sync", lambda e: e.dma_start(
        stg[:, 0:16].rearrange("p (a b) -> p a b", b=2),
        d["L1_attn_w_gate"][:].rearrange("(k p) n -> p k n", p=128)),
        reads=[("dr:L1_attn_w_gate", 0, 1)], writes=[stg.rng()], inc=16)
    b.emit("vector", lambda e: e.tensor_tensor(
        wgc.h[:, 0:8].rearrange("p (a b) -> p a b", b=1),
        stg[:, 0:16].rearrange("p (a b) -> p a b", b=2)[:, :, 0:1],
        stg[:, 0:16].rearrange("p (a b) -> p a b", b=2)[:, :, 1:2],
        ALU.subtract),
        reads=[stg.rng()], writes=[wgc.rng()])
    b.emit("vector", lambda e: e.tensor_copy(bcols.h[:, 16:24],
                                             wgc.h[:, 0:8]),
           reads=[wgc.rng()], writes=[bcols.rng()])
    b.emit("sync", lambda e: e.dma_start(
        stg[0:1, 0:2], d["L1_attn_b_gate"][:].rearrange("(a n) -> a n", a=1)),
        reads=[("dr:L1_attn_b_gate", 0, 1)], writes=[stg.rng()], inc=16)
    b.emit("vector", lambda e: e.tensor_tensor(
        stg[0:1, 2:3], stg[0:1, 0:1], stg[0:1, 1:2], ALU.subtract),
        reads=[stg.rng()], writes=[stg.rng()])
    b.emit("tensor", lambda e: e.matmul(
        pC[:, 0:1], onesf[0:1, 0:128], stg[0:1, 2:3], start=True, stop=True),
        reads=[stg.rng(), onesf.rng()], writes=[pCrng])
    b.emit("vector", lambda e: e.tensor_copy(colsC.h[:, 4:5], pC[:, 0:1]),
           reads=[pCrng], writes=[colsC.rng()])
    b.emit("vector", lambda e: e.tensor_scalar_mul(colsC.h[:, 5:6],
                                                   colsC.h[:, 4:5], -1.0),
           reads=[colsC.rng()], writes=[colsC.rng()])

    rrslot = [0]

    def regen_seq(s, items):
        infos = []
        for (t, cc, dst_ap, dst_rng) in items:
            sl = rrslot[0] % 8
            psl = rrslot[0] % 2
            rrslot[0] += 1
            infos.append((t, cc, dst_ap, dst_rng, sl, psl))
        pbf = pC[:].bitcast(BF16)

        def st0(i):
            (t, cc, dst_ap, dst_rng, sl, psl) = infos[i]
            srng = stream.rng((NT * s + t) * 4096, (NT * s + t + 1) * 4096)
            mc = mstats[:, 2 * t:2 * t + 1]
            rc = mstats[:, 2 * t + 1:2 * t + 2]
            xr = ("sb", xn.lo + 256 * sl, xn.lo + 256 * (sl + 1))
            pr = ("ps", pCrng[1] + 2048 * psl, pCrng[1] + 2048 * (psl + 1))
            b.emit("vector", lambda e, t=t, cc=cc, mc=mc, rc=rc, sl=sl:
                   e.tensor_scalar(
                       xn[:, 128 * sl:128 * (sl + 1)],
                       stream[:, NT * s + t, 128 * cc:128 * (cc + 1)],
                       mc, rc, ALU.subtract, ALU.mult),
                   reads=[srng, mstats.rng()], writes=[xr])
            b.emit("tensor", lambda e, sl=sl, psl=psl: e.transpose(
                pbf[:, 1024 * psl:1024 * psl + 128],
                xn[:, 128 * sl:128 * (sl + 1)], identb[:]),
                reads=[xr, identb.rng()], writes=[pr])

        def st1(i):
            (t, cc, dst_ap, dst_rng, sl, psl) = infos[i]
            pr = ("ps", pCrng[1] + 2048 * psl, pCrng[1] + 2048 * (psl + 1))
            b.emit("vector", lambda e, dst_ap=dst_ap, psl=psl:
                   e.tensor_copy(dst_ap,
                                 pbf[:, 1024 * psl:1024 * psl + 128]),
                   reads=[pr], writes=[dst_rng])
        stagger(len(infos), st0, st1)

    def regen_local_strip(s, st):
        regen_seq(s, [(2 * st + i, cc,
                       s0f[:, 256 * cc + 128 * i:256 * cc + 128 * (i + 1)],
                       s0r(256 * cc + 128 * i, 256 * cc + 128 * (i + 1)))
                      for i in range(2) for cc in range(4)])

    def load_512(nm, dst_ap, dst_rng, rows, cols):
        b.emit("sync", lambda e: e.dma_start(
            stg[:, 0:512], d[nm][rows, cols]),
            reads=[("dr:" + nm, 0, 1)], writes=[stg.rng()], inc=16)
        b.emit("vector", lambda e: e.tensor_copy(dst_ap, stg[:, 0:512]),
               reads=[stg.rng()], writes=[dst_rng])

    def attn_head_sbuf(h, nkt, qlen, lkT, lkT_rng, ext0, out_dst, out_rng,
                       pscr, pscr_rng, lqT, q0):
        for kt in range(nkt):
            qoff = max(0, 128 * kt - q0)
            qw = qlen - qoff
            b.emit("tensor", lambda e, kt=kt, qoff=qoff, qw=qw: e.matmul(
                pA[:, 0:qw], lkT[:, 128 * kt:128 * (kt + 1)],
                lqT[:, qoff:qoff + qw], start=True, stop=True),
                reads=[lkT_rng, S0RNG], writes=[pArng])
            b.emit("scalar", lambda e, qw=qw: e.activation(
                pscr[:, 0:qw], pA[:, 0:qw], AF.Exp, bias=0.0, scale=0.125),
                reads=[pArng], writes=[pscr_rng])
            if 128 * kt >= q0:
                b.emit("vector", lambda e: e.tensor_tensor(
                    pscr[:, 0:128], pscr[:, 0:128], maskb[:], ALU.mult),
                    reads=[pscr_rng, maskb.rng()], writes=[pscr_rng])
            b.emit("tensor", lambda e, kt=kt, qw=qw: e.matmul(
                pB[0:65, 0:qw],
                w2f[:, ext0 + 520 * kt + 65 * h:ext0 + 520 * kt + 65 * (h + 1)],
                pscr[:, 0:qw], start=True, stop=True),
                reads=[W2RNG, pscr_rng], writes=[pBrng])
            if kt == 0:
                b.emit("vector", lambda e, qw=qw: e.tensor_copy(
                    facc[0:65, 0:qw], pB[0:65, 0:qw]),
                    reads=[pBrng], writes=[facc.rng()])
            else:
                b.emit("vector", lambda e, qoff=qoff, qw=qw: e.tensor_tensor(
                    facc[0:65, qoff:qoff + qw], facc[0:65, qoff:qoff + qw],
                    pB[0:65, 0:qw], ALU.add),
                    reads=[facc.rng(), pBrng], writes=[facc.rng()])
        b.emit("vector", lambda e: e.reciprocal(
            facc[64:65, 0:qlen], facc[64:65, 0:qlen]),
            reads=[facc.rng()], writes=[facc.rng()])
        b.emit("tensor", lambda e: e.matmul(
            pC[0:64, 0:qlen], onesf[64:65, 0:64], facc[64:65, 0:qlen],
            start=True, stop=True),
            reads=[facc.rng(), onesf.rng()], writes=[pCrng])
        b.emit("vector", lambda e: e.tensor_tensor(
            out_dst, facc[0:64, 0:qlen], pC[0:64, 0:qlen], ALU.mult),
            reads=[facc.rng(), pCrng], writes=[out_rng])

    def attn_head(h, nkt, qlen, lkT, lkT_rng, ext0, out_dst, out_rng,
                  pscr, pscr_rng, lqT):
        q0 = 128 * nkt - qlen
        if not PV_PSUM:
            return attn_head_sbuf(h, nkt, qlen, lkT, lkT_rng, ext0, out_dst,
                                  out_rng, pscr, pscr_rng, lqT, q0)
        hb = h % 2
        pvrng = ("ps", pBrng[1] + 2048 * hb, pBrng[1] + 2048 * (hb + 1))
        pv = pB[:, 512 * hb:512 * (hb + 1)]
        pw = 256 if qlen <= 256 else 512
        nslots = 1024 // pw

        def _geo(kt):
            sb = kt % 2
            psl = kt % nslots
            return (sb,
                    ("ps", pArng[1] + 2048 * sb, pArng[1] + 2048 * (sb + 1)),
                    ("sb", pscr_rng[1] + 2 * pw * psl,
                     pscr_rng[1] + 2 * pw * (psl + 1)),
                    pscr[:, pw * psl:pw * psl + qlen],
                    max(0, 128 * kt - q0))

        def st_s(kt):
            sb, srng_, prng_, ps, qoff = _geo(kt)
            qw = qlen - qoff
            b.emit("tensor", lambda e, kt=kt, qoff=qoff, qw=qw, sb=sb:
                   e.matmul(
                       pA[:, 512 * sb:512 * sb + qw],
                       lkT[:, 128 * kt:128 * (kt + 1)],
                       lqT[:, qoff:qoff + qw],
                       start=True, stop=True),
                   reads=[lkT_rng, S0RNG], writes=[srng_])

        def st_e(kt):
            sb, srng_, prng_, ps, qoff = _geo(kt)
            qw = qlen - qoff
            b.emit("scalar", lambda e, qw=qw, sb=sb, ps=ps: e.activation(
                ps[:, 0:qw], pA[:, 512 * sb:512 * sb + qw], AF.Exp,
                bias=0.0, scale=0.125),
                reads=[srng_], writes=[prng_])
            if 128 * kt >= q0:
                b.emit("vector", lambda e, ps=ps: e.tensor_tensor(
                    ps[:, 0:128], ps[:, 0:128], maskb[:], ALU.mult),
                    reads=[prng_, maskb.rng()], writes=[prng_])

        def st_pv(kt):
            sb, srng_, prng_, ps, qoff = _geo(kt)
            qw = qlen - qoff
            b.emit("tensor", lambda e, kt=kt, qw=qw, qoff=qoff, ps=ps:
                   e.matmul(
                       pv[0:65, qoff:qoff + qw],
                       w2f[:, ext0 + 520 * kt + 65 * h:
                           ext0 + 520 * kt + 65 * (h + 1)],
                       ps[:, 0:qw],
                       start=(kt == 0), stop=(kt == nkt - 1)),
                   reads=[W2RNG, prng_], writes=[pvrng])
        stagger(nkt, st_s, st_e, st_pv)
        fsl = 512 * hb
        frng = ("sb", facc.lo + 4 * fsl, facc.lo + 4 * (fsl + qlen))
        b.emit("vector", lambda e: e.reciprocal(
            facc[64:65, fsl:fsl + qlen], pv[64:65, 0:qlen]),
            reads=[pvrng], writes=[frng])
        b.emit("tensor", lambda e: e.matmul(
            pC[0:64, 0:qlen], onesf[64:65, 0:64],
            facc[64:65, fsl:fsl + qlen],
            start=True, stop=True),
            reads=[frng, onesf.rng()], writes=[pCrng])
        b.emit("vector", lambda e: e.tensor_copy(
            facc[0:64, fsl:fsl + qlen], pC[0:64, 0:qlen]),
            reads=[pCrng], writes=[frng])
        b.emit("vector", lambda e: e.tensor_tensor(
            out_dst, pv[0:64, 0:qlen], facc[0:64, fsl:fsl + qlen], ALU.mult),
            reads=[pvrng, frng], writes=[out_rng])

    pscr = fs0.h[:].bitcast(BF16)
    pscr_rng = fs0.rng()

    for s in range(NSEQ):
        for t in range(NT):
            srng = stream.rng((NT * s + t) * 4096, (NT * s + t + 1) * 4096)
            m, r, lrng = ln_stats(c, scol, stream[:, NT * s + t, :], srng,
                                  slot=c.lnslot())
            b.emit("vector", lambda e, m=m, t=t: e.tensor_copy(
                mstats[:, 2 * t:2 * t + 1], m),
                reads=[lrng], writes=[mstats.rng()])
            b.emit("vector", lambda e, r=r, t=t: e.tensor_copy(
                mstats[:, 2 * t + 1:2 * t + 2], r),
                reads=[lrng], writes=[mstats.rng()])

        kvf = kvb.h[:].rearrange("p a b -> p (a b)")
        obank = [("ps", pArng[1], pArng[1] + 2048),
                 ("ps", pArng[1] + 2048, pArng[1] + 4096),
                 ("ps", pBrng[1], pBrng[1] + 2048),
                 ("ps", pBrng[1] + 2048, pBrng[1] + 4096)]
        for cc in range(8):
            regen_seq(s, [(t, cc, s0f[:, 128 * t:128 * (t + 1)],
                           s0r(128 * t, 128 * (t + 1))) for t in range(NT)])
            for r4 in range(4):
                kci = r4 * 8 + cc
                ks = r4 % 2
                krng = ("sb", kvb.lo + 1024 * ks, kvb.lo + 1024 * (ks + 1))
                b.emit("sync", lambda e, kci=kci: e.dma_start(
                    stg[:, 0:512],
                    d["L1_attn_w_comp"][128 * kci:128 * (kci + 1), 512:1024]),
                    reads=[("dr:L1_attn_w_comp", 0, 1)], writes=[stg.rng()],
                    inc=16)
                b.emit("vector", lambda e, ks=ks: e.tensor_copy(
                    kvf[:, 512 * ks:512 * (ks + 1)], stg[:, 0:512]),
                    reads=[stg.rng()], writes=[krng])
                for o in range(4):
                    px = pA if o < 2 else pB
                    b.emit("tensor", lambda e, o=o, r4=r4, px=px, ks=ks:
                           e.matmul(
                               px[:, 512 * (o % 2):512 * (o % 2) + 512],
                               kvf[:, 512 * ks + 128 * o:
                                   512 * ks + 128 * (o + 1)],
                               s0f[:, r4:2048:4],
                               start=(r4 == 0), stop=(r4 == 3)),
                           reads=[krng, S0RNG], writes=[obank[o]])
            for (acc, px, r0, r1) in ((facc, pA, 0, 2), (fout, pB, 2, 4)):
                if cc == 0:
                    b.emit("vector", lambda e, acc=acc, px=px:
                           e.tensor_copy(acc[:], px[:]),
                           reads=[obank[r0], obank[r1 - 1]],
                           writes=[acc.rng()])
                else:
                    b.emit("vector", lambda e, acc=acc, px=px:
                           e.tensor_tensor(acc[:], acc[:], px[:], ALU.add),
                           reads=[acc.rng(), obank[r0], obank[r1 - 1]],
                           writes=[acc.rng()])
        for o in range(4):
            acc = facc if o < 2 else fout
            b.emit("scalar", lambda e, o=o, acc=acc: e.activation(
                w2f[:, CGOFF + 512 * o:CGOFF + 512 * (o + 1)],
                acc[:, 512 * (o % 2):512 * (o % 2 + 1)],
                AF.Identity, bias=bcomp_c[:, o:o + 1], scale=1.0),
                reads=[acc.rng(), colsA.rng()], writes=[W2RNG])

        for cc4 in range(4):
            load_512("L1_attn_w_gqkv",
                     w1f[:, 1536 * cc4:1536 * cc4 + 512], W1RNG,
                     slice(128 * cc4, 128 * (cc4 + 1)), slice(0, 512))
            load_512("L1_attn_w_gqkv",
                     w1f[:, 1536 * cc4 + 512:1536 * cc4 + 1024], W1RNG,
                     slice(128 * cc4, 128 * (cc4 + 1)), slice(512, 1024))
            load_512("L1_attn_w_gqkv",
                     w1f[:, 1536 * cc4 + 1024:1536 * (cc4 + 1)], W1RNG,
                     slice(128 * cc4, 128 * (cc4 + 1)), slice(1024, 1536))
        for o in range(12):
            for cc4 in range(4):
                b.emit("tensor", lambda e, o=o, cc4=cc4: e.matmul(
                    pA[:, 0:512],
                    w1f[:, 1536 * cc4 + 128 * o:1536 * cc4 + 128 * (o + 1)],
                    w2f[:, CGOFF + 512 * cc4:CGOFF + 512 * (cc4 + 1)],
                    start=(cc4 == 0), stop=(cc4 == 3)),
                    reads=[W1RNG, W2RNG], writes=[pArng])
            if o < 4:
                dst, drng = s0f[:, 512 * o:512 * (o + 1)], S0RNG
            elif o < 8:
                dst, drng = s1f[:, 512 * (o - 4):512 * (o - 3)], S1RNG
            else:
                dst, drng = w0f[:, 512 * (o - 8):512 * (o - 7)], W0RNG
            b.emit("scalar", lambda e, o=o, dst=dst: e.activation(
                dst, pA[:, 0:512], AF.Identity, bias=colsB.h[:, o:o + 1],
                scale=1.0),
                reads=[pArng, colsB.rng()], writes=[drng])
        for kt in range(4):
            b.emit("vector", lambda e, kt=kt: e.memset(
                w2f[:, 520 * kt:520 * (kt + 1)]
                .rearrange("p (a b) -> p a b", b=65)[:, :, 64:65], 1.0),
                writes=[w2r(520 * kt, 520 * (kt + 1))])
        pbf = pC[:].bitcast(BF16)
        for vch in range(4):
            for kt in range(4):
                b.emit("tensor", lambda e, vch=vch, kt=kt, pbf=pbf:
                       e.transpose(pbf[:, 0:128],
                                   w0f[:, 512 * vch + 128 * kt:
                                       512 * vch + 128 * (kt + 1)],
                                   identb[:]),
                       reads=[W0RNG, identb.rng()], writes=[pCrng])
                b.emit("vector", lambda e, vch=vch, kt=kt, pbf=pbf:
                       e.tensor_copy(
                           w2f[:, 520 * kt + 130 * vch:
                               520 * kt + 130 * (vch + 1)]
                           .rearrange("p (a b) -> p a b", b=65)[:, :, 0:64],
                           pbf[:, 0:128].rearrange("p (a b) -> p a b", b=64)),
                       reads=[pCrng],
                       writes=[w2r(520 * kt + 130 * vch,
                                   520 * kt + 130 * (vch + 1))])

        for h in range(8):
            row = 64 * (h % 2)
            ch = h // 2
            lqT = s0f[:, 512 * ch:512 * (ch + 1)][row:row + 64, :]
            lkT = s1f[:, 512 * ch:512 * (ch + 1)][row:row + 64, :]
            out_dst = s0f[:, 512 * ch:512 * (ch + 1)][row:row + 64, :]
            attn_head(h, 4, 512, lkT, S1RNG, 0, out_dst,
                      s0r(512 * ch, 512 * (ch + 1)),
                      pscr, pscr_rng, lqT)

        for cc4 in range(4):
            load_512("L1_attn_w_gproj",
                     w1f[:, 6144 + 512 * cc4:6144 + 512 * (cc4 + 1)], W1RNG,
                     slice(128 * cc4, 128 * (cc4 + 1)), slice(0, 512))
        for o in range(4):
            for cc4 in range(4):
                b.emit("tensor", lambda e, o=o, cc4=cc4: e.matmul(
                    pA[:, 0:512],
                    w1f[:, 6144 + 512 * cc4 + 128 * o:
                        6144 + 512 * cc4 + 128 * (o + 1)],
                    s0f[:, 512 * cc4:512 * (cc4 + 1)],
                    start=(cc4 == 0), stop=(cc4 == 3)),
                    reads=[W1RNG, S0RNG], writes=[pArng])
            b.emit("scalar", lambda e, o=o: e.activation(
                w2f[:, CGOFF + 512 * o:CGOFF + 512 * (o + 1)], pA[:, 0:512],
                AF.Identity, bias=colsC.h[:, o:o + 1], scale=1.0),
                reads=[pArng, colsC.rng()], writes=[W2RNG])

        for cc4 in range(4):
            load_512("L1_attn_w_lqkv",
                     w1f[:, 1536 * cc4:1536 * cc4 + 512], W1RNG,
                     slice(128 * cc4, 128 * (cc4 + 1)), slice(0, 512))
            load_512("L1_attn_w_lqkv",
                     w1f[:, 1536 * cc4 + 512:1536 * cc4 + 1024], W1RNG,
                     slice(128 * cc4, 128 * (cc4 + 1)), slice(512, 1024))
            load_512("L1_attn_w_lqkv",
                     w1f[:, 1536 * cc4 + 1024:1536 * (cc4 + 1)], W1RNG,
                     slice(128 * cc4, 128 * (cc4 + 1)), slice(1024, 1536))
        for kt in range(16):
            b.emit("vector", lambda e, kt=kt: e.memset(
                w2f[:, 520 * kt:520 * (kt + 1)]
                .rearrange("p (a b) -> p a b", b=65)[:, :, 64:65], 1.0),
                writes=[w2r(520 * kt, 520 * (kt + 1))])
        b.emit("sync", lambda e: e.dma_start(
            stg[0:1, 0:512],
            d["L1_attn_b_lqkv"][:].rearrange("(a n) -> a n", a=1)
            [:, 1024:1536]),
            reads=[("dr:b_lqkv", 0, 1)], writes=[stg.rng()], inc=16)
        b.emit("tensor", lambda e: e.matmul(
            pD[:, 0:512], onesf[0:1, 0:128], stg[0:1, 0:512],
            start=True, stop=True),
            reads=[stg.rng(), onesf.rng()], writes=[pDrng])
        b.emit("vector", lambda e: e.tensor_copy(fs0[:], pD[:, 0:512]),
               reads=[pDrng], writes=[fs0.rng()])
        for st in range(8):
            regen_local_strip(s, st)
            for o in range(4):
                for cc4 in range(4):
                    b.emit("tensor", lambda e, o=o, cc4=cc4: e.matmul(
                        pA[:, 0:256],
                        w1f[:, 1536 * cc4 + 512 + 128 * o:
                            1536 * cc4 + 512 + 128 * (o + 1)],
                        s0f[:, 256 * cc4:256 * (cc4 + 1)],
                        start=(cc4 == 0), stop=(cc4 == 3)),
                        reads=[W1RNG, S0RNG], writes=[pArng])
                b.emit("scalar", lambda e, o=o, st=st: e.activation(
                    w0f[:, 2048 * o + 256 * st:2048 * o + 256 * (st + 1)],
                    pA[:, 0:256], AF.Identity, bias=colsB.h[:, 4 + o:5 + o],
                    scale=1.0),
                    reads=[pArng, colsB.rng()],
                    writes=[w0r(2048 * o + 256 * st,
                                2048 * o + 256 * (st + 1))])
            for i in range(2):
                kt = 2 * st + i
                for cc4 in range(4):
                    b.emit("tensor", lambda e, i=i, cc4=cc4: e.matmul(
                        pB[:, 0:512],
                        s0f[:, 256 * cc4 + 128 * i:256 * cc4 + 128 * (i + 1)],
                        w1f[:, 1536 * cc4 + 1024:1536 * (cc4 + 1)],
                        start=(cc4 == 0), stop=(cc4 == 3)),
                        reads=[W1RNG, S0RNG], writes=[pBrng])
                b.emit("vector", lambda e, kt=kt: e.scalar_tensor_tensor(
                    w2f[:, 520 * kt:520 * (kt + 1)]
                    .rearrange("p (a b) -> p a b", b=65)[:, :, 0:64],
                    pB[:, 0:512].rearrange("p (a b) -> p a b", b=64),
                    1.0, fs0.h[:].rearrange("p (a b) -> p a b", b=64),
                    ALU.mult, ALU.add),
                    reads=[pBrng, fs0.rng()],
                    writes=[w2r(520 * kt, 520 * (kt + 1))])

        for cc4 in range(4):
            load_512("L1_attn_w_lproj",
                     w1f[:, 6144 + 512 * cc4:6144 + 512 * (cc4 + 1)], W1RNG,
                     slice(128 * cc4, 128 * (cc4 + 1)), slice(0, 512))
        kvf = kvb.h[:].rearrange("p a b -> p (a b)")
        for st in range(8):
            regen_local_strip(s, st)
            for o in range(4):
                for cc4 in range(4):
                    b.emit("tensor", lambda e, o=o, cc4=cc4: e.matmul(
                        pA[:, 0:256],
                        w1f[:, 1536 * cc4 + 128 * o:
                            1536 * cc4 + 128 * (o + 1)],
                        s0f[:, 256 * cc4:256 * (cc4 + 1)],
                        start=(cc4 == 0), stop=(cc4 == 3)),
                        reads=[W1RNG, S0RNG], writes=[pArng])
                b.emit("scalar", lambda e, o=o: e.activation(
                    s0f[:, 1024 + 256 * o:1024 + 256 * (o + 1)],
                    pA[:, 0:256], AF.Identity, bias=colsA.h[:, o:o + 1],
                    scale=1.0),
                    reads=[pArng, colsA.rng()],
                    writes=[s0r(1024 + 256 * o, 1024 + 256 * (o + 1))])
            for h in range(8):
                row = 64 * (h % 2)
                ch = h // 2
                lqT = s0f[:, 1024 + 256 * ch:1024 + 256 * (ch + 1)][
                    row:row + 64, :]
                lkT = w0f[:, 2048 * ch:2048 * (ch + 1)][row:row + 64, :]
                out_dst = s1f[:, 256 * ch:256 * (ch + 1)][row:row + 64, :]
                attn_head(h, 2 * st + 2, 256, lkT, W0RNG, 0, out_dst,
                          s1r(256 * ch, 256 * (ch + 1)),
                          pscr, pscr_rng, lqT)
            for o in range(4):
                for cc4 in range(4):
                    b.emit("tensor", lambda e, o=o, cc4=cc4: e.matmul(
                        pA[:, 0:256],
                        w1f[:, 6144 + 512 * cc4 + 128 * o:
                            6144 + 512 * cc4 + 128 * (o + 1)],
                        s1f[:, 256 * cc4:256 * (cc4 + 1)],
                        start=(cc4 == 0), stop=(cc4 == 3)),
                        reads=[W1RNG, S1RNG], writes=[pArng])
                b.emit("scalar", lambda e, o=o: e.activation(
                    s1f[:, 1024 + 256 * o:1024 + 256 * (o + 1)],
                    pA[:, 0:256], AF.Identity, bias=colsB.h[:, 12 + o:13 + o],
                    scale=1.0),
                    reads=[pArng, colsB.rng()],
                    writes=[s1r(1024 + 256 * o, 1024 + 256 * (o + 1))])
            for i in range(2):
                t = 2 * st + i
                srng = stream.rng((NT * s + t) * 4096,
                                  (NT * s + t + 1) * 4096)
                for cc4 in range(4):
                    b.emit("vector", lambda e, cc4=cc4, t=t: e.tensor_copy(
                        kvf[:, 128 * cc4:128 * (cc4 + 1)]
                        .rearrange("p (a b) -> p a b", b=4),
                        w2f[:, CGOFF + 512 * cc4 + 32 * t:
                            CGOFF + 512 * cc4 + 32 * (t + 1)]
                        .rearrange("p (a b) -> p a b", b=1)
                        .to_broadcast([128, 32, 4])),
                        reads=[W2RNG], writes=[kvb.rng()])
                for cc in range(8):
                    if cc < 4:
                        lh = s1f[:, 1024 + 256 * cc + 128 * i:
                                 1024 + 256 * cc + 128 * (i + 1)]
                        lrng = S1RNG
                    else:
                        lh = kvf[:, 128 * (cc - 4):128 * (cc - 3)]
                        lrng = kvb.rng()
                    b.emit("tensor", lambda e, cc=cc, lh=lh, lrng=lrng:
                           e.matmul(
                               pC[:, 0:1], lh, bcols.h[:, 16 + cc:17 + cc],
                               start=(cc == 0), stop=(cc == 7)),
                           reads=[lrng, bcols.rng()], writes=[pCrng])
                b.emit("scalar", lambda e: e.activation(
                    colsC.h[:, 6:7], pC[:, 0:1], AF.Exp,
                    bias=colsC.h[:, 5:6], scale=-1.0),
                    reads=[pCrng, colsC.rng()], writes=[colsC.rng()])
                b.emit("vector", lambda e: e.tensor_scalar(
                    colsC.h[:, 6:7], colsC.h[:, 6:7], 1.0, None, ALU.add),
                    reads=[colsC.rng()], writes=[colsC.rng()])
                b.emit("vector", lambda e: e.reciprocal(
                    colsC.h[:, 6:7], colsC.h[:, 6:7]),
                    reads=[colsC.rng()], writes=[colsC.rng()])
                b.emit("vector", lambda e: e.tensor_scalar(
                    colsC.h[:, 7:8], colsC.h[:, 6:7], -1.0, 1.0,
                    ALU.mult, ALU.add),
                    reads=[colsC.rng()], writes=[colsC.rng()])
                pbf2 = pD[:].bitcast(BF16)
                for cc4 in range(4):
                    b.emit("tensor", lambda e, cc4=cc4, i=i, pbf2=pbf2:
                           e.transpose(
                               pbf2[:, 128 * cc4:128 * (cc4 + 1)],
                               s1f[:, 1024 + 256 * cc4 + 128 * i:
                                   1024 + 256 * cc4 + 128 * (i + 1)],
                               identb[:]),
                           reads=[S1RNG, identb.rng()], writes=[pDrng])
                for cc4 in range(4):
                    b.emit("tensor", lambda e, cc4=cc4, pbf2=pbf2:
                           e.transpose(
                               pbf2[:, 512 + 128 * cc4:512 + 128 * (cc4 + 1)],
                               kvf[:, 128 * cc4:128 * (cc4 + 1)],
                               identb[:]),
                           reads=[kvb.rng(), identb.rng()], writes=[pDrng])
                fo = fout if i == 0 else facc
                forng = fout.rng() if i == 0 else facc.rng()
                b.emit("vector", lambda e, pbf2=pbf2, fo=fo, forng=forng:
                       e.tensor_scalar(
                           fo[:, 0:512], pbf2[:, 0:512], colsC.h[:, 6:7],
                           None, ALU.mult),
                       reads=[pDrng, colsC.rng()], writes=[forng])
                b.emit("vector", lambda e, pbf2=pbf2, fo=fo, forng=forng:
                       e.tensor_scalar(
                           fo[:, 512:1024], pbf2[:, 512:1024],
                           colsC.h[:, 7:8], None, ALU.mult),
                       reads=[pDrng, colsC.rng()], writes=[forng])
                m, r, lrng = ln_stats(c, scol, fo[:], forng,
                                      slot=c.lnslot())
                b.emit("vector", lambda e, m=m, r=r, fo=fo, forng=forng:
                       e.tensor_scalar(fo[:], fo[:], m, r,
                                       ALU.subtract, ALU.mult),
                       reads=[forng, lrng], writes=[forng])
                b.emit("vector", lambda e, s=s, t=t, fo=fo, forng=forng:
                       e.tensor_tensor(
                           stream[:, NT * s + t, :],
                           stream[:, NT * s + t, :], fo[:], ALU.add),
                       reads=[srng, forng], writes=[srng])


EMITTERS["l1attn"] = emit_l1attn



def emit_l1ffn(c, stream, d):
    b, ar, pool, nc = c.b, c.ar, c.pool, c.nc
    identb, onesf = c.identb, c.onesf
    scol = ar.shared("scol", [128, 64], F32)
    xn = ar.shared("xn", [128, 1024], BF16)
    stg = ar.shared("stg", [128, 512], F32)
    facc = ar.shared("facc", [128, 1024], F32)
    fout = ar.shared("fout", [128, 1024], F32)
    fs0 = ar.shared("fs0", [128, 512], F32)
    kvb = ar.shared("kvb", [128, 8, 128], BF16)
    strip0 = ar.shared("strip0", [128, 8, 256], BF16)
    wb1 = ar.shared("wb1", [128, 8, 1024], BF16)
    qcols = ar.shared("qcols", [128, 96], F32)
    kvf32 = Buf(kvb.h[:].rearrange("p a b -> p (a b)").bitcast(F32),
                "kvb", kvb.lo, kvb.hi)

    pA, pArng = pool.get(0)
    pC, pCrng = pool.get(2)

    L2 = float(LN2)

    def quantize_tile(src, src_rng, dst, dst_rng):
        b.emit("vector", lambda e: e.tensor_reduce(
            qcols.h[:, 32:64], src.rearrange("p (a b) -> p a b", b=32),
            AX.X, ALU.max, apply_absolute_value=True),
            reads=[src_rng], writes=[qcols.rng()])
        b.emit("vector", lambda e: e.tensor_scalar(
            qcols.h[:, 32:64], qcols.h[:, 32:64], 1.0 / 7.5, 1e-30,
            ALU.mult, ALU.max),
            reads=[qcols.rng()], writes=[qcols.rng()])
        b.emit("vector", lambda e: e.reciprocal(
            qcols.h[:, 0:32], qcols.h[:, 32:64]),
            reads=[qcols.rng()], writes=[qcols.rng()])
        b.emit("vector", lambda e: e.tensor_tensor(
            src.rearrange("p (a b) -> p a b", b=32),
            src.rearrange("p (a b) -> p a b", b=32),
            qcols.h[:, 0:32].rearrange("p (a b) -> p a b", b=1)
            .to_broadcast([128, 32, 32]), ALU.mult),
            reads=[src_rng, qcols.rng()], writes=[src_rng])
        for q4 in range(4):
            sl = slice(256 * q4, 256 * (q4 + 1))
            yq = src[:, sl]
            b.emit("scalar", lambda e, yq=yq: e.activation(
                fs0[:, 0:256], yq, AF.Abs, bias=0.0, scale=1.0),
                reads=[src_rng], writes=[fs0.rng()])
            b.emit("vector", lambda e: e.tensor_single_scalar(
                fs0[:, 256:512], fs0[:, 0:256], 3.875, ALU.is_gt),
                reads=[fs0.rng()], writes=[fs0.rng()])
            b.emit("vector", lambda e: e.scalar_tensor_tensor(
                kvf32.h[:, 0:256], fs0[:, 0:256], 1.9375, fs0[:, 256:512],
                ALU.is_gt, ALU.add),
                reads=[fs0.rng()], writes=[kvf32.rng()])
            bp = c.cbias(3.0 * L2)
            bn = c.cbias(-3.0 * L2)
            b.emit("scalar", lambda e, bp=bp: e.activation(
                fs0[:, 0:256], kvf32.h[:, 0:256], AF.Exp, bias=bp,
                scale=-L2),
                reads=[kvf32.rng(), c.constb.rng()], writes=[fs0.rng()])
            b.emit("scalar", lambda e, bn=bn: e.activation(
                kvf32.h[:, 256:512], kvf32.h[:, 0:256], AF.Exp,
                bias=bn, scale=L2),
                reads=[kvf32.rng(), c.constb.rng()], writes=[kvf32.rng()])
            b.emit("vector", lambda e, yq=yq: e.tensor_tensor(
                yq, yq, fs0[:, 0:256], ALU.mult),
                reads=[src_rng, fs0.rng()], writes=[src_rng])
            b.emit("vector", lambda e, yq=yq: e.tensor_scalar(
                yq, yq, MAGIC, MAGIC, ALU.add, ALU.subtract),
                reads=[src_rng], writes=[src_rng])
            b.emit("vector", lambda e, yq=yq: e.tensor_tensor(
                yq, yq, kvf32.h[:, 256:512], ALU.mult),
                reads=[src_rng, kvf32.rng()], writes=[src_rng])
            b.emit("vector", lambda e, yq=yq, q4=q4: e.tensor_tensor(
                yq.rearrange("p (a b) -> p a b", b=32),
                yq.rearrange("p (a b) -> p a b", b=32),
                qcols.h[:, 32 + 8 * q4:40 + 8 * q4]
                .rearrange("p (a b) -> p a b", b=1)
                .to_broadcast([128, 8, 32]), ALU.mult),
                reads=[src_rng, qcols.rng()], writes=[src_rng])
        b.emit("vector", lambda e: e.tensor_copy(dst[:, 0:1024], src),
               reads=[src_rng], writes=[dst_rng])

    for hf in range(2):
        b.emit("sync", lambda e, hf=hf: e.dma_start(
            stg[0:1, 0:512], d["L1_ffn_bias"][:].rearrange("(a n) -> a n",
                                                           a=1)
            [:, 512 * hf:512 * (hf + 1)]),
            reads=[("dr:L1_ffn_bias", 0, 1)], writes=[stg.rng()], inc=16)
        b.emit("tensor", lambda e, hf=hf: e.matmul(
            pC[:, 0:512], onesf[0:1, 0:128], stg[0:1, 0:512],
            start=True, stop=True),
            reads=[stg.rng(), onesf.rng()], writes=[pCrng])
        b.emit("vector", lambda e, hf=hf: e.tensor_copy(
            facc[:, 512 * hf:512 * (hf + 1)], pC[:, 0:512]),
            reads=[pCrng], writes=[facc.rng()])

    for kc in range(8):
        for hf in range(2):
            b.emit("sync", lambda e, kc=kc, hf=hf: e.dma_start(
                stg[:], d["L1_ffn_weight"][128 * kc:128 * (kc + 1),
                                           512 * hf:512 * (hf + 1)]),
                reads=[("dr:L1_ffn_weight", 0, 1)], writes=[stg.rng()],
                inc=16)
            b.emit("vector", lambda e, hf=hf: e.tensor_copy(
                fout[:, 512 * hf:512 * (hf + 1)], stg[:]),
                reads=[stg.rng()], writes=[fout.rng()])
        if kc == 0:
            c.tap("wf0", fout.h, fout.rng(), 1024)
        quantize_tile(fout[:], fout.rng(), xn, xn.rng())
        if kc == 0:
            c.tap("wd0", xn.h, xn.rng(), 1024)
            c.tap("wq0", fout.h, fout.rng(), 1024)
            c.tap("qc0", qcols.h, qcols.rng(), 96)
        pbf = pC[:].bitcast(BF16)
        for cc in range(8):
            b.emit("tensor", lambda e, cc=cc, pbf=pbf: e.transpose(
                pbf[:, 128 * cc:128 * (cc + 1)],
                xn[:, 128 * cc:128 * (cc + 1)], identb[:]),
                reads=[xn.rng(), identb.rng()], writes=[pCrng])
        b.emit("vector", lambda e, kc=kc, pbf=pbf: e.tensor_copy(
            wb1.h[:, :, 128 * kc:128 * (kc + 1)],
            pbf[:, 0:1024].rearrange("p (a b) -> p a b", b=128)),
            reads=[pCrng], writes=[wb1.rng()])
    c.tap("wdt", wb1.h[:].rearrange("p a b -> p (a b)"), wb1.rng(), 8192)

    for s in range(NSEQ):
        for t in range(NT):
            srng = stream.rng((NT * s + t) * 4096, (NT * s + t + 1) * 4096)
            m, r, lrng = ln_stats(c, scol, stream[:, NT * s + t, :], srng,
                                  slot=c.lnslot())
            b.emit("vector", lambda e, m=m, r=r, s=s, t=t: e.tensor_scalar(
                fout[:], stream[:, NT * s + t, :], m, r,
                ALU.subtract, ALU.mult),
                reads=[srng, lrng], writes=[fout.rng()])
            quantize_tile(fout[:], fout.rng(), xn, xn.rng())
            if s == 0 and t == 0:
                c.tap("xd0", xn.h, xn.rng(), 1024)
            pbf = pC[:].bitcast(BF16)
            for cc in range(8):
                b.emit("tensor", lambda e, cc=cc, pbf=pbf: e.transpose(
                    pbf[:, 128 * cc:128 * (cc + 1)],
                    xn[:, 128 * cc:128 * (cc + 1)], identb[:]),
                    reads=[xn.rng(), identb.rng()], writes=[pCrng])
            b.emit("vector", lambda e, pbf=pbf: e.tensor_copy(
                strip0.h[:, :, 0:128],
                pbf[:, 0:1024].rearrange("p (a b) -> p a b", b=128)),
                reads=[pCrng], writes=[strip0.rng()])
            for hf in range(2):
                for cc in range(8):
                    b.emit("tensor", lambda e, hf=hf, cc=cc: e.matmul(
                        pA[:, 512 * hf:512 * (hf + 1)],
                        strip0.h[:, cc, 0:128],
                        wb1.h[:, cc, 512 * hf:512 * (hf + 1)],
                        start=(cc == 0), stop=(cc == 7)),
                        reads=[strip0.rng(), wb1.rng()], writes=[pArng])
            b.emit("vector", lambda e: e.tensor_tensor(
                fout[:], pA[:], facc[:], ALU.add),
                reads=[pArng, facc.rng()], writes=[fout.rng()])
            if s == 0 and t == 0:
                c.tap("o0", fout.h, fout.rng(), 1024)
            m, r, lrng = ln_stats(c, scol, fout[:], fout.rng(), slot=c.lnslot())
            b.emit("vector", lambda e, m=m, r=r: e.tensor_scalar(
                fout[:], fout[:], m, r, ALU.subtract, ALU.mult),
                reads=[fout.rng(), lrng], writes=[fout.rng()])
            b.emit("vector", lambda e, s=s, t=t: e.tensor_tensor(
                stream[:, NT * s + t, :], stream[:, NT * s + t, :],
                fout[:], ALU.add),
                reads=[srng, fout.rng()], writes=[srng])


EMITTERS["l1ffn"] = emit_l1ffn


# version 14
# speedup vs baseline: 1.1043x; 1.1043x over previous
import numpy as np
import ml_dtypes
import concourse.bass as bass
import concourse.mybir as mybir
from concourse.bass_utils import run_bass_kernel_spmd

F32 = mybir.dt.float32
BF16 = mybir.dt.bfloat16
AF = mybir.ActivationFunctionType
ALU = mybir.AluOpType
AX = mybir.AxisListType

MAGIC = float(np.float32(1.5 * 2 ** 23))
LN2 = float(np.log(2.0))
NCORES = 8
SEQ = 2048
NSEQ = 2
D = 1024
NT = SEQ // 128
EPS_LN = 1e-5

ENGINES = ("sync", "tensor", "vector", "scalar", "gpsimd")
DMAV = ("dma0", "dma1", "dma2", "dma3")
TRACKS = ENGINES + DMAV


class Buf:

    def __init__(self, handle, name, lo, hi):
        self.h = handle
        self.name = name
        self.lo = lo
        self.hi = hi

    def __getitem__(self, key):
        return self.h[key]

    def rng(self, felo=None, fehi=None):
        if felo is None:
            return ("sb", self.lo, self.hi)
        return ("sb", self.lo + felo, min(self.hi, self.lo + fehi))


class Arena:

    def __init__(self, nc):
        self.nc = nc
        self.base = (nc.sbuf_base + 63) // 64 * 64
        self.limit = nc.sbuf_top - self.base
        self.off = 0
        self.marks = []
        self.n = 0

    def alloc(self, name, shape, dtype, align=32):
        bpe = 2 if dtype == BF16 else 4
        nbytes = int(np.prod(shape[1:])) * bpe
        off = (self.off + align - 1) // align * align
        assert off + nbytes <= self.limit, (
            f"SBUF overflow: {name} at {off}+{nbytes} > {self.limit}")
        h = self.nc.alloc_sbuf_tensor_at(
            f"{name}_{self.n}", shape, dtype, offset=self.base + off)
        self.n += 1
        self.off = off + nbytes
        return Buf(h, name, off, off + nbytes)

    def push(self):
        self.marks.append(self.off)

    def pop(self):
        self.off = self.marks.pop()

    def shared(self, name, shape, dtype):
        if not hasattr(self, "_shared"):
            self._shared = {}
        key = name
        if key in self._shared:
            buf, shp, dt = self._shared[key]
            assert tuple(shp) == tuple(shape) and dt == dtype,                 f"shared buf {name} shape/dtype mismatch"
            return buf
        buf = self.alloc(name, shape, dtype)
        self._shared[key] = (buf, tuple(shape), dtype)
        return buf


class B:

    def __init__(self, nc):
        self.nc = nc
        self.sem = {e: nc.alloc_semaphore(f"sem_{e}") for e in TRACKS}
        self.cnt = {e: 0 for e in TRACKS}
        self.ops = {e: [] for e in ENGINES}
        self.last_wait = {a: {b: 0 for b in TRACKS} for a in TRACKS}
        self.writes = []
        self.reads = []
        self.nemit = 0
        self.drain_after = {e: set() for e in ENGINES}
        self.dmarr = 0

    def _overlap(self, recs, space, lo, hi):
        out = {}
        for (s, l, h, e, c) in recs:
            if s == space and l < hi and h > lo:
                if c > out.get(e, -1):
                    out[e] = c
        return out

    SERIAL = False

    def emit(self, eng, fn, reads=(), writes=(), inc=1):
        queue = eng
        if inc == 16:
            eng = DMAV[self.dmarr % 4]
            self.dmarr += 1
        need = {}
        if B.SERIAL:
            for e in TRACKS:
                if self.cnt[e] > 0:
                    need[e] = self.cnt[e]
        for (sp, lo, hi) in reads:
            for e, c in self._overlap(self.writes, sp, lo, hi).items():
                if c > need.get(e, 0):
                    need[e] = c
        for (sp, lo, hi) in writes:
            for e, c in self._overlap(self.writes, sp, lo, hi).items():
                if c > need.get(e, 0):
                    need[e] = c
            for e, c in self._overlap(self.reads, sp, lo, hi).items():
                if e == eng and queue != "sync":
                    continue
                if c > need.get(e, 0):
                    need[e] = c
        if inc == 16:
            if self.cnt[eng] > need.get(eng, 0):
                need[eng] = self.cnt[eng]
        waits = []
        for e, c in need.items():
            if e == eng and queue != "sync" and eng == "tensor":
                continue
            if c > self.last_wait[eng][e]:
                waits.append((e, c))
                self.last_wait[eng][e] = c
        after = self.cnt[eng] + inc
        self.ops[queue].append((waits, fn, inc, eng))
        self.cnt[eng] = after
        for (sp, lo, hi) in writes:
            self.writes.append((sp, lo, hi, eng, after))
        for (sp, lo, hi) in reads:
            self.reads.append((sp, lo, hi, eng, after))
        self.nemit += 1
        if self.nemit % 4096 == 0:
            self._gc()

    def _gc(self):
        floor = {e: min(min(self.last_wait[a][e] for a in TRACKS if a != e),
                        self.cnt[e] if (e in ("sync", "tensor") or e in DMAV)
                        else (self.cnt[e] // 64) * 64)
                 for e in TRACKS}

        def keep(rec):
            return rec[4] > floor[rec[3]]
        self.writes = [r for r in self.writes if keep(r)]
        self.reads = [r for r in self.reads if keep(r)]

    def barrier(self):
        snapshot = dict(self.cnt)
        for eng in ENGINES:
            waits = []
            for e in TRACKS:
                if e != eng and snapshot[e] > self.last_wait[eng][e]:
                    waits.append((e, snapshot[e]))
                    self.last_wait[eng][e] = snapshot[e]
            if waits:
                self.ops[eng].append((waits, None, 0, eng))

    def finish(self, eng="sync"):
        waits = []
        for e in TRACKS:
            if self.cnt[e] > self.last_wait[eng][e]:
                waits.append((e, self.cnt[e]))
                self.last_wait[eng][e] = self.cnt[e]
        self.ops[eng].append((waits, None, 0, eng))

    def build(self):
        awaited = {e: set() for e in TRACKS}
        for eng in ENGINES:
            for (waits, fn, inc, trk) in self.ops[eng]:
                for (we, wc) in waits:
                    awaited[we].add(wc)
        dscr = getattr(self, "dscr", None)

        def dummy_fn(eng):
            if dscr is None:
                return None
            if eng in ("vector", "gpsimd"):
                return lambda e: e.memset(dscr[:, 0:1], 0.0)
            if eng == "scalar":
                return lambda e: e.activation(dscr[:, 1:2], dscr[:, 0:1],
                                              AF.Identity, bias=0.0,
                                              scale=0.0)
            return None

        with self.nc.Block() as blk:
            for eng in ENGINES:
                lst = self.ops[eng]
                if not lst:
                    continue

                def mk(lst, eng):
                    aw = awaited[eng]
                    dummy = dummy_fn(eng)

                    def body(e):
                        pend = 0
                        need_pub = False
                        cnt = 0
                        for waits, fn, inc, trk in lst:
                            if need_pub and (waits or fn is None
                                             or inc == 16
                                             or eng == "tensor"):
                                if dummy is not None and eng != "tensor":
                                    dummy(e).then_inc(self.sem[eng], pend)
                                else:
                                    e.drain(fusable=True).then_inc(
                                        self.sem[eng], pend)
                                pend = 0
                                need_pub = False
                            for (we, wc) in waits:
                                e.wait_ge(self.sem[we], wc)
                            if fn is None:
                                continue
                            n0 = self.nc.n_instructions()
                            ins = fn(e)
                            if inc == 16:
                                assert self.nc.n_instructions() - n0 == 1, \
                                    "dma_start split into multiple instrs"
                                ins.then_inc(self.sem[trk], 16)
                                continue
                            if eng == "tensor":
                                cnt += inc
                                pend += inc
                                if cnt in aw:
                                    ins.then_inc(self.sem[eng], pend)
                                    pend = 0
                                continue
                            if need_pub:
                                ins.then_inc(self.sem[eng], pend)
                                pend = 0
                                need_pub = False
                            cnt += inc
                            pend += inc
                            if cnt in aw:
                                need_pub = True
                            if cnt % 64 == 0:
                                d = e.drain(fusable=True)
                                if need_pub:
                                    d.then_inc(self.sem[eng], pend)
                                    pend = 0
                                    need_pub = False
                        if pend:
                            e.drain(fusable=True).then_inc(self.sem[eng],
                                                           pend)
                    return body
                getattr(blk, eng)(mk(lst, eng))


import heapq


class Sched:

    COST = {"tensor": (90, 1.0 / 9.6), "vector": (400, 0.26),
            "scalar": (300, 0.21), "gpsimd": (400, 0.5),
            "sync": (1800, 0.32)}
    DEP_LAT = 150.0

    def __init__(self, nc):
        self.nc = nc
        self.rec = []
        self.writes = {}
        self.reads = {}
        self.cnt = {e: 0 for e in TRACKS}

    def emit(self, queue, fn, reads=(), writes=(), inc=1):
        idx = len(self.rec)
        deps = set()
        for (sp, lo, hi) in reads:
            for (l, h, j) in self.writes.get(sp, ()):
                if l < hi and h > lo:
                    deps.add(j)
        for (sp, lo, hi) in writes:
            for recs in (self.writes, self.reads):
                for (l, h, j) in recs.get(sp, ()):
                    if l < hi and h > lo:
                        deps.add(j)
        for (sp, lo, hi) in writes:
            wl = self.writes.setdefault(sp, [])
            wl[:] = [(l, h, j) for (l, h, j) in wl
                     if not (lo <= l and h <= hi)]
            wl.append((lo, hi, idx))
            rl = self.reads.setdefault(sp, [])
            rl[:] = [(l, h, j) for (l, h, j) in rl
                     if not (lo <= l and h <= hi)]
        for (sp, lo, hi) in reads:
            self.reads.setdefault(sp, []).append((lo, hi, idx))
        self.rec.append((queue, fn, inc, tuple(reads), tuple(writes), deps))

    def barrier(self):
        idx = len(self.rec)
        deps = set(range(idx))
        self.rec.append(("__barrier__", None, 0, (), (), deps))

    def finish(self, eng="sync"):
        self.barrier()

    def _schedule(self):
        n = len(self.rec)
        children = [[] for _ in range(n)]
        ndep = [0] * n
        barrier_pending = []
        for i, (q, fn, inc, r, w, deps) in enumerate(self.rec):
            ndep[i] = len(deps)
            for j in deps:
                children[j].append(i)
        qavail = {}
        finish_t = [0.0] * n
        ready_t = [0.0] * n
        heap = []
        for i in range(n):
            if ndep[i] == 0:
                heapq.heappush(heap, (0.0, i))
        order = []
        pending = [0.0] * n
        while heap:
            _, i = heapq.heappop(heap)
            (q, fn, inc, r, w, deps) = self.rec[i]
            if q == "__barrier__":
                order.append(i)
                st = max([pending[i]] + list(qavail.values()) + [0.0])
                finish_t[i] = st
            else:
                fixed, perb = self.COST[q]
                span = sum(hi - lo for (sp, lo, hi) in w) or 64
                st = max(qavail.get(q, 0.0), pending[i])
                finish_t[i] = st + fixed + perb * min(span, 16384)
                qavail[q] = finish_t[i]
                order.append(i)
            for ch in children[i]:
                pending[ch] = max(pending[ch], finish_t[i] + self.DEP_LAT)
                ndep[ch] -= 1
                if ndep[ch] == 0:
                    heapq.heappush(heap, (pending[ch], ch))
        assert len(order) == n, "cycle in dep graph?"
        return order

    def build(self):
        order = self._schedule()
        b = B(self.nc)
        b.dscr = getattr(self, "dscr", None)
        for i in order:
            (q, fn, inc, r, w, deps) = self.rec[i]
            if q == "__barrier__":
                b.barrier()
            else:
                b.emit(q, fn, reads=list(r), writes=list(w), inc=inc)
        b.finish()
        b.build()



def ln_stats(c, scr, x_ap, x_rng, m_col=None, rstd_col=None, scol_rng=None,
             slot=0):
    b = c.b
    s = scr
    k = 16 * (slot % 4)
    rng = ("sb", scr.lo + 64 * (slot % 4), scr.lo + 64 * (slot % 4 + 1))
    eps = c.cbias(EPS_LN)
    b.emit("vector", lambda e, x=x_ap, o=s[:, k:k + 6]:
           e.bn_stats(o, x[:, 0:512]),
           reads=[x_rng], writes=[rng])
    b.emit("vector", lambda e, x=x_ap, o=s[:, k + 6:k + 12]:
           e.bn_stats(o, x[:, 512:1024]),
           reads=[x_rng], writes=[rng])
    b.emit("vector", lambda e, i=s[:, k:k + 12], o=s[:, k + 12:k + 14]:
           e.bn_aggr(o, i.rearrange("p (a b) -> p a b", b=6)),
           reads=[rng], writes=[rng])
    b.emit("scalar", lambda e, i=s[:, k + 13:k + 14], o=s[:, k + 14:k + 15]:
           e.activation(o, i, AF.Ln, bias=eps, scale=1.0),
           reads=[rng, c.constb.rng()], writes=[rng])
    b.emit("scalar", lambda e, i=s[:, k + 14:k + 15], o=s[:, k + 15:k + 16]:
           e.activation(o, i, AF.Exp, bias=0.0, scale=-0.5),
           reads=[rng], writes=[rng])
    return s[:, k + 12:k + 13], s[:, k + 15:k + 16], rng


def ln_stats_nmr(c, scr, x_ap, x_rng, slot=0):
    m, r, rng = ln_stats(c, scr, x_ap, x_rng, slot=slot)
    k = 16 * (slot % 4)
    nmr = scr[:, k + 14:k + 15]
    c.b.emit("vector", lambda e, nmr=nmr, m=m, r=r:
             e.scalar_tensor_tensor(nmr, m, -1.0, r, ALU.mult, ALU.mult),
             reads=[rng], writes=[rng])
    return m, r, nmr, rng


def stagger(n, *stages):
    ns = len(stages)
    for k in range(n + ns - 1):
        for sidx in range(ns - 1, -1, -1):
            i = k - sidx
            if 0 <= i < n:
                stages[sidx](i)


def make_ident_and_masks(b, nc, ar, d_const):
    stg = ar.shared("stg", [128, 512], F32)
    cst = Buf(stg.h, "stg", stg.lo, stg.lo + 1024)
    b.emit("sync", lambda e: e.dma_start(cst[0:128, 0:256], d_const[:]),
           reads=[("dr:cconst", 0, 1)], writes=[cst.rng()], inc=16)
    identb = ar.alloc("identb", [128, 128], BF16)
    maskb = ar.alloc("maskb", [128, 128], BF16)
    onesb = ar.alloc("onesb", [128, 128], BF16)
    onesf = ar.alloc("onesf", [128, 128], F32)
    b.emit("vector", lambda e: e.tensor_copy(identb[:], cst.h[:, 0:128]),
           reads=[cst.rng()], writes=[identb.rng()])
    b.emit("vector", lambda e: e.tensor_copy(maskb[:], cst.h[:, 128:256]),
           reads=[cst.rng()], writes=[maskb.rng()])
    b.emit("vector", lambda e: e.memset(onesb[:], 1.0), writes=[onesb.rng()])
    b.emit("vector", lambda e: e.memset(onesf[:], 1.0), writes=[onesf.rng()])
    return cst, identb, maskb, onesb, onesf


def vec_to_cols(c, d_vec, tag, out, rowscr, psum, n=None, col0=0):
    b = c.b
    if n is None:
        n = out.h.shape[1] * 128
    k = n // 128
    pX, pXrng = psum
    for j0 in range(0, n, 512):
        w = min(512, n - j0)
        b.emit("sync", lambda e, j0=j0, w=w: e.dma_start(
            rowscr[0:1, 0:w],
            d_vec[:].rearrange("(a n) -> a n", a=1)[:, j0:j0 + w]),
            reads=[("dr:" + tag, 0, 1)], writes=[rowscr.rng()], inc=16)
        for j in range(w // 128):
            b.emit("tensor", lambda e, j=j, j0=j0: e.matmul(
                pX[:, j0 // 128 + j:j0 // 128 + j + 1],
                rowscr[0:1, 128 * j:128 * (j + 1)],
                c.onesf[0:1, 0:1], start=True, stop=True),
                reads=[rowscr.rng(), c.onesf.rng()], writes=[pXrng])
    b.emit("vector", lambda e: e.tensor_copy(out.h[:, col0:col0 + k],
                                             pX[:, 0:k]),
           reads=[pXrng], writes=[out.rng()])


def load_weight_bf16(b, ar, d_w, name, rows, cols, tag):
    k = rows // 128
    wf = ar.alloc(name + "_f", [128, k, cols], F32)
    b.emit("sync", lambda e: e.dma_start(
        wf[:], d_w[:].rearrange("(k p) n -> p k n", p=128)),
        reads=[("dr:" + tag, 0, 1)], writes=[wf.rng()], inc=16)
    wb = ar.alloc(name + "_b", [128, k, cols], BF16)
    tot = k * cols
    step = 8192
    for off in range(0, tot, step):
        n = min(step, tot - off)
        b.emit("vector", lambda e, off=off, n=n: e.tensor_copy(
            wb.h[:].rearrange("p a b -> p (a b)")[:, off:off + n],
            wf.h[:].rearrange("p a b -> p (a b)")[:, off:off + n]),
            reads=[wf.rng()], writes=[wb.rng()])
    return wf, wb


def bias_cols(b, ar, d_bias, name, n, tag):
    k = n // 128
    bb = ar.alloc(name, [128, k], F32)
    b.emit("sync", lambda e: e.dma_start(
        bb[:], d_bias[:].rearrange("(k p) -> p k", p=128)),
        reads=[("dr:" + tag, 0, 1)], writes=[bb.rng()], inc=16)
    return bb


def bias_bcast(b, ar, psum, d_bias, name, n, tag, onesf):
    row = ar.alloc(name + "_row", [1, n], F32)
    b.emit("sync", lambda e: e.dma_start(
        row[:], d_bias[:].rearrange("(a n) -> a n", a=1)),
        reads=[("dr:" + tag, 0, 1)], writes=[row.rng()], inc=16)
    out = ar.alloc(name + "_bc", [128, n], F32)
    pa, prng = psum.get(0)
    for j in range(0, n, 512):
        w = min(512, n - j)
        b.emit("tensor", lambda e, j=j, w=w: e.matmul(
            pa[:, 0:w], onesf[0:1, 0:128], row[0:1, j:j + w],
            start=True, stop=True),
            reads=[row.rng(), onesf.rng()], writes=[prng])
        b.emit("vector", lambda e, j=j, w=w: e.tensor_copy(
            out[:, j:j + w], pa[:, 0:w]),
            reads=[prng], writes=[out.rng()])
    return out


class PsumPool:

    def __init__(self, nc):
        self.t = [nc.alloc_psum_tensor(f"pp{i}", [128, 1024], F32)
                  for i in range(4)]

    def get(self, i):
        return self.t[i], ("ps", i * 4096, (i + 1) * 4096)

    def half(self, i, h):
        return (self.t[i][:, h * 512:(h + 1) * 512],
                ("ps", i * 4096 + h * 2048, i * 4096 + (h + 1) * 2048))

    def bf16(self, i):
        return (self.t[i][:].bitcast(BF16),
                ("ps", i * 4096, (i + 1) * 4096))



def regen_xn_strip(c, stream, s, st0, ntile, mstats, xn, xnstrip, identb,
                   pbank, slot0=0):
    b, pool = c.b, c.pool
    pT, pTrng = pbank
    base = pTrng[1]
    pbf = pT[:].bitcast(BF16)

    def st0f(i):
        t = st0 + i
        par = t % 2
        prng = ("ps", base + 2048 * par, base + 2048 * (par + 1))
        srng = stream.rng((NT * s + t) * 4096, (NT * s + t + 1) * 4096)
        mc = mstats[:, 2 * t:2 * t + 1]
        rc = mstats[:, 2 * t + 1:2 * t + 2]
        b.emit("vector", lambda e, o=xn[:], x=stream[:, NT * s + t, :],
               mc=mc, rc=rc:
               e.tensor_scalar(o, x, mc, rc, ALU.subtract, ALU.mult),
               reads=[srng, mstats.rng()], writes=[xn.rng()])
        for cc in range(8):
            b.emit("tensor", lambda e, cc=cc, par=par:
                   e.transpose(pbf[:, 1024 * par + 128 * cc:
                                   1024 * par + 128 * (cc + 1)],
                               xn[:, 128 * cc:128 * (cc + 1)], identb[:]),
                   reads=[xn.rng(), identb.rng()], writes=[prng])

    def st1f(i):
        t = st0 + i
        par = t % 2
        prng = ("ps", base + 2048 * par, base + 2048 * (par + 1))
        i2 = i + slot0
        _, xlo, _ = xnstrip.rng()
        wr = [("sb", xlo + 512 * cc + 256 * i2, xlo + 512 * cc + 256 * (i2 + 1))
              for cc in range(8)]
        b.emit("vector", lambda e, i=i2, par=par:
               e.tensor_copy(
                   xnstrip.h[:, :, 128 * i:128 * (i + 1)],
                   pbf[:, 1024 * par:1024 * (par + 1)]
                   .rearrange("p (a b) -> p a b", b=128)),
               reads=[prng], writes=wr)
    stagger(ntile, st0f, st1f)



WEIGHT_SPECS = {
    "l0attn": [("L0_attn_Wq", (D, D)), ("L0_attn_bq", (D,)),
               ("L0_attn_Wk", (D, D)), ("L0_attn_bk", (D,)),
               ("L0_attn_Wv", (D, D)), ("L0_attn_bv", (D,))],
    "l0ffn": [("L0_ffn_W_r", (D, D)), ("L0_ffn_W_k", (D, D)),
              ("L0_ffn_W_v", (D, D)), ("L0_ffn_mu_r", (D,)),
              ("L0_ffn_mu_k", (D,))],
    "l1attn": [("L1_attn_w_lqkv", (512, 1536)), ("L1_attn_b_lqkv", (1536,)),
               ("L1_attn_w_gqkv", (512, 1536)), ("L1_attn_b_gqkv", (1536,)),
               ("L1_attn_w_comp", (4096, D)), ("L1_attn_b_comp", (D,)),
               ("L1_attn_w_lproj", (512, 512)), ("L1_attn_b_lproj", (512,)),
               ("L1_attn_w_gproj", (512, 512)), ("L1_attn_b_gproj", (512,)),
               ("L1_attn_w_gate", (D, 2)), ("L1_attn_b_gate", (2,))],
    "l1ffn": [("L1_ffn_weight", (D, D)), ("L1_ffn_bias", (D,))],
}

EMITTERS = {}


DEBUG_TAPS = []
TRUNC = 0
NOPOP = 0
PV_PSUM = True
CG_PSUM = False


class Ctx:
    def __init__(self, nc):
        self.nc = nc
        self.b = Sched(nc)
        self.ar = Arena(nc)
        self.pool = PsumPool(nc)
        self.constb = self.ar.alloc("constb", [128, 8], F32)
        self._cvals = {}
        self.dbg = None
        self.dbgcur = 0
        self.dbgmap = {}
        if DEBUG_TAPS:
            self.d_dbg = nc.dram_tensor("dbgout", [128, 16384], F32,
                                        kind="ExternalOutput")
            self.dbg = self.ar.alloc("dbgbuf", [128, 512], F32)

    def tap(self, tag, ap, rng, ncols):
        if self.dbg is None or tag not in DEBUG_TAPS:
            return
        assert self.dbgcur + ncols <= 16384, "debug overflow"
        cur = self.dbgcur
        self.dbgmap[tag] = (cur, ncols)
        self.dbgcur += ncols
        for off in range(0, ncols, 512):
            n = min(512, ncols - off)
            self.b.emit("vector", lambda e, ap=ap, off=off, n=n:
                        e.tensor_copy(self.dbg[:, 0:n], ap[:, off:off + n]),
                        reads=[rng], writes=[self.dbg.rng()])
            self.b.emit("sync", lambda e, cur=cur, off=off, n=n: e.dma_start(
                self.d_dbg[:, cur + off:cur + off + n], self.dbg[:, 0:n]),
                reads=[self.dbg.rng()], writes=[("dr:dbg", cur + off,
                                                 cur + off + n)], inc=16)

    def lnslot(self):
        self._lns = getattr(self, "_lns", 0) + 1
        return self._lns % 4

    def cbias(self, value):
        v = float(value)
        if v not in self._cvals:
            assert len(self._cvals) < 8
            j = len(self._cvals)
            self._cvals[v] = j
            self.b.emit("vector", lambda e, j=j, v=v: e.memset(
                self.constb[:, j:j + 1], v), writes=[self.constb.rng()])
        return self.constb[:, self._cvals[v]:self._cvals[v] + 1]


def make_consts():
    ident = np.eye(128, dtype=np.float32)
    mask = (np.arange(128)[:, None] <= np.arange(128)[None, :]
            ).astype(np.float32)
    return np.concatenate([ident, mask], axis=1)


def build_program(blocks):
    nc = bass.Bass("TRN2", target_bir_lowering=False)
    c = Ctx(nc)
    b, ar = c.b, c.ar
    d = {}
    d["xs"] = nc.dram_tensor("xs", [NSEQ, SEQ, D], F32, kind="ExternalInput")
    d["cconst"] = nc.dram_tensor("cconst", [128, 256], F32,
                                 kind="ExternalInput")
    for blk in blocks:
        for (nm, shp) in WEIGHT_SPECS[blk]:
            d[nm] = nc.dram_tensor(nm, list(shp), F32, kind="ExternalInput")
    d_y = nc.dram_tensor("ys", [NSEQ, SEQ, D], F32, kind="ExternalOutput")

    stream = ar.alloc("stream", [128, NSEQ * NT, 1024], F32)
    b.dscr = ar.shared("bdummy", [128, 8], F32).h
    c.cst, c.identb, c.maskb, c.onesb, c.onesf = make_ident_and_masks(
        b, nc, ar, d["cconst"])

    for st in range(NSEQ * NT):
        s, t = divmod(st, NT)
        b.emit("sync", lambda e, s=s, t=t, st=st: e.dma_start(
            stream.h[:, st, :], d["xs"][s, 128 * t:128 * (t + 1), :]),
            reads=[("dr:xs", 0, 1)],
            writes=[stream.rng(4096 * st, 4096 * (st + 1))], inc=16)

    for blk in blocks:
        EMITTERS[blk](c, stream, d)

    for st in range(NSEQ * NT):
        s, t = divmod(st, NT)
        b.emit("sync", lambda e, s=s, t=t, st=st: e.dma_start(
            d_y[s, 128 * t:128 * (t + 1), :], stream.h[:, st, :]),
            reads=[stream.rng(4096 * st, 4096 * (st + 1))],
            writes=[("dr:ys", st, st + 1)], inc=16)
    b.build()
    return nc


_PROG_CACHE = {}


def get_program(blocks):
    key = tuple(blocks)
    if key not in _PROG_CACHE:
        _PROG_CACHE[key] = build_program(list(blocks))
    return _PROG_CACHE[key]


FUSED = True
_BLOCK_ORDER = ["l0attn", "l0ffn", "l1attn", "l1ffn"]


def kernel(**inputs):
    consts = make_consts()
    x = np.ascontiguousarray(inputs["L0_attn_x"], dtype=np.float32)
    launches = ([_BLOCK_ORDER] if FUSED
                else [[blk] for blk in _BLOCK_ORDER])
    cur = x
    for blocks in launches:
        nc = get_program(blocks)
        in_maps = []
        for core in range(NCORES):
            m = {"xs": cur[2 * core:2 * core + 2], "cconst": consts}
            for blk in blocks:
                for (nm, _) in WEIGHT_SPECS[blk]:
                    m[nm] = inputs[nm]
            in_maps.append(m)
        res = run_bass_kernel_spmd(nc, in_maps, core_ids=list(range(NCORES)))
        cur = np.concatenate([np.asarray(r["ys"]) for r in res.results],
                             axis=0)
    return cur.reshape(16, SEQ, D)





def emit_l0attn(c, stream, d):
    b, ar, pool, nc = c.b, c.ar, c.pool, c.nc
    identb, onesb = c.identb, c.onesb
    mstats = ar.shared("mstats", [128, 32], F32)
    scol = ar.shared("scol", [128, 64], F32)
    xn = ar.shared("xn", [128, 1024], BF16)
    kvb = ar.shared("kvb", [128, 8, 128], BF16)
    bcolsx = ar.shared("bcols", [128, 32], BF16)
    ksb = Buf(bcolsx.h[:, 24:32], "bcols", bcolsx.lo + 48, bcolsx.lo + 64)
    stg = ar.shared("stg", [128, 512], F32)
    kvacc = ar.shared("facc", [128, 1024], F32)
    fout = ar.shared("fout", [128, 1024], F32)
    ksacc = ar.shared("cols0", [128, 8], F32)
    scr2 = ar.shared("fs0", [128, 512], F32)
    ao = kvacc
    zr = ar.shared("cols1", [128, 8], F32)
    rowf = Buf(stg.h, "stg", stg.lo, stg.hi)
    bkr = Buf(fout.h[0:1, 0:512].bitcast(BF16), "fout", fout.lo, fout.lo + 2048)
    bvr = Buf(fout.h[0:1, 512:1024].bitcast(BF16), "fout",
              fout.lo + 2048, fout.lo + 4096)
    bqc = ar.shared("cols2", [128, 8], F32)
    nbqc = ar.shared("cols3", [128, 8], F32)
    strip0 = ar.shared("strip0", [128, 8, 256], BF16)
    strip1 = ar.shared("strip1", [128, 8, 256], BF16)
    s1flat = strip1.h[:].rearrange("p a b -> p (a b)")
    vtm = Buf(s1flat[:, 0:1024], "strip1", strip1.lo, strip1.lo + 2048)
    ktm = Buf(s1flat[:, 1024:2048], "strip1", strip1.lo + 2048,
              strip1.lo + 4096)
    wb0 = ar.shared("wb0", [128, 8, 1024], BF16)
    wb1 = ar.shared("wb1", [128, 8, 1024], BF16)

    def load_w(nm, wb):
        for kc in range(8):
            for hf in range(2):
                b.emit("sync", lambda e, nm=nm, kc=kc, hf=hf: e.dma_start(
                    stg[:], d[nm][128 * kc:128 * (kc + 1),
                                  512 * hf:512 * (hf + 1)]),
                    reads=[("dr:" + nm, 0, 1)], writes=[stg.rng()], inc=16)
                b.emit("vector", lambda e, wb=wb, kc=kc, hf=hf: e.tensor_copy(
                    wb.h[:, kc, 512 * hf:512 * (hf + 1)], stg[:]),
                    reads=[stg.rng()], writes=[wb.rng()])

    for (nm, buf) in (("L0_attn_bk", bkr), ("L0_attn_bv", bvr)):
        for hf in range(2):
            b.emit("sync", lambda e, nm=nm, hf=hf: e.dma_start(
                rowf[0:1, 0:512], d[nm][:].rearrange("(a n) -> a n", a=1)
                [:, 512 * hf:512 * (hf + 1)]),
                reads=[("dr:" + nm, 0, 1)], writes=[rowf.rng()], inc=16)
            b.emit("vector", lambda e, buf=buf, hf=hf: e.tensor_copy(
                buf.h[:, 512 * hf:512 * (hf + 1)], rowf[0:1, 0:512]),
                reads=[rowf.rng()], writes=[buf.rng()])
    vec_to_cols(c, d["L0_attn_bq"], "L0_attn_bq", bqc, rowf, c.pool.get(0))
    b.emit("vector", lambda e: e.tensor_scalar_mul(nbqc[:], bqc[:], -1.0),
           reads=[bqc.rng()], writes=[nbqc.rng()])

    for s in range(NSEQ):
        for t in range(NT):
            srng = stream.rng((NT * s + t) * 4096, (NT * s + t + 1) * 4096)
            m, r, lrng = ln_stats(c, scol, stream[:, NT * s + t, :], srng,
                                  slot=c.lnslot())
            b.emit("vector", lambda e, m=m, t=t: e.tensor_copy(
                mstats[:, 2 * t:2 * t + 1], m),
                reads=[lrng], writes=[mstats.rng()])
            b.emit("vector", lambda e, r=r, t=t: e.tensor_copy(
                mstats[:, 2 * t + 1:2 * t + 2], r),
                reads=[lrng], writes=[mstats.rng()])

        load_w("L0_attn_Wk", wb0)
        load_w("L0_attn_Wv", wb1)
        pA, pArng = pool.get(0)
        pB, pBrng = pool.get(1)
        pC, pCrng = pool.get(2)
        pD, pDrng = pool.get(3)
        b.emit("vector", lambda e: e.memset(kvacc[:], 0.0),
               writes=[kvacc.rng()])
        b.emit("vector", lambda e: e.memset(ksacc[:], 0.0),
               writes=[ksacc.rng()])

        def p1_regen(t):
            regen_xn_strip(c, stream, s, t, 1, mstats, xn, strip0, identb,
                           pool.get(2), slot0=t % 2)

        def p1_mm(t):
            q = t % 2
            for (wb, brow, pX, pXrng2) in ((wb0, bkr, pA, pArng),
                                           (wb1, bvr, pB, pBrng)):
                for h in range(2):
                    for cc in range(8):
                        b.emit("tensor",
                               lambda e, wb=wb, h=h, cc=cc, pX=pX, q=q:
                               e.matmul(pX[:, 512 * h:512 * (h + 1)],
                                        strip0.h[:, cc,
                                                 128 * q:128 * (q + 1)],
                                        wb.h[:, cc, 512 * h:512 * (h + 1)],
                                        start=(cc == 0), stop=False),
                               reads=[strip0.rng(), wb.rng()],
                               writes=[pXrng2])
                    b.emit("tensor", lambda e, brow=brow, h=h, pX=pX:
                           e.matmul(pX[:, 512 * h:512 * (h + 1)],
                                    onesb[0:1, 0:128],
                                    brow[0:1, 512 * h:512 * (h + 1)],
                                    start=False, stop=True),
                           reads=[brow.rng(), onesb.rng()], writes=[pXrng2])

        def p1_tail(t):
            b.emit("vector", lambda e: e.tensor_copy(vtm.h, pB[:]),
                   reads=[pBrng], writes=[vtm.rng()])
            for q4 in range(4):
                sl = slice(256 * q4, 256 * (q4 + 1))
                b.emit("scalar", lambda e, sl=sl: e.activation(
                    scr2[:, 0:256], pA[:, sl], AF.Relu, bias=0.0,
                    scale=-1.0),
                    reads=[pArng], writes=[scr2.rng()])
                b.emit("scalar", lambda e: e.activation(
                    scr2[:, 256:512], scr2[:, 0:256], AF.Exp, bias=0.0,
                    scale=-1.0),
                    reads=[scr2.rng()], writes=[scr2.rng()])
                b.emit("vector", lambda e, sl=sl: e.tensor_scalar_max(
                    scr2[:, 0:256], pA[:, sl], 0.0),
                    reads=[pArng], writes=[scr2.rng()])
                b.emit("vector", lambda e, sl=sl: e.tensor_tensor(
                    ktm.h[:, sl], scr2[:, 256:512], scr2[:, 0:256],
                    ALU.add),
                    reads=[scr2.rng()], writes=[ktm.rng()])
            for hh in range(8):
                b.emit("tensor", lambda e, hh=hh: e.matmul(
                    pD[:, 128 * hh:128 * (hh + 1)],
                    ktm.h[:, 128 * hh:128 * (hh + 1)],
                    vtm.h[:, 128 * hh:128 * (hh + 1)],
                    start=True, stop=True),
                    reads=[ktm.rng(), vtm.rng()], writes=[pDrng])
                b.emit("tensor", lambda e, hh=hh: e.matmul(
                    pC[:, 512 + hh:512 + hh + 1],
                    ktm.h[:, 128 * hh:128 * (hh + 1)],
                    onesb[:, 0:1],
                    start=True, stop=True),
                    reads=[ktm.rng(), onesb.rng()],
                    writes=[("ps", 2 * 4096 + 2048, 3 * 4096)])
            b.emit("vector", lambda e: e.tensor_tensor(
                kvacc[:], kvacc[:], pD[:], ALU.add),
                reads=[kvacc.rng(), pDrng], writes=[kvacc.rng()])
            b.emit("vector", lambda e: e.tensor_tensor(
                ksacc[:], ksacc[:], pC[:, 512:520], ALU.add),
                reads=[ksacc.rng(), ("ps", 2 * 4096 + 2048, 3 * 4096)],
                writes=[ksacc.rng()])
        stagger(NT, p1_regen, p1_mm, p1_tail)
        b.emit("vector", lambda e: e.tensor_copy(
            kvb.h[:].rearrange("p a b -> p (a b)"), kvacc[:]),
            reads=[kvacc.rng()], writes=[kvb.rng()])
        b.emit("vector", lambda e: e.tensor_copy(ksb.h, ksacc[:]),
               reads=[ksacc.rng()], writes=[ksb.rng()])

        load_w("L0_attn_Wq", wb0)
        halves = [pool.half(0, 0), pool.half(0, 1),
                  pool.half(1, 0), pool.half(1, 1)]
        wb2x = ar.shared("wb2", [128, 10368], BF16)
        stA = [strip0,
               Buf(wb2x.h[:, 0:2048].rearrange("p (a b) -> p a b", b=256),
                   "wb2", wb2x.lo, wb2x.lo + 4096)]
        stQ = [strip1,
               Buf(wb2x.h[:, 2048:4096].rearrange("p (a b) -> p a b", b=256),
                   "wb2", wb2x.lo + 4096, wb2x.lo + 8192)]

        def p2_regen(st):
            regen_xn_strip(c, stream, s, 2 * st, 2, mstats, xn,
                           stA[st % 2], identb, pool.get(2))

        def p2_qproj(st):
            sa, sq = stA[st % 2], stQ[st % 2]
            for o in range(8):
                ph, phrng = halves[o % 4]
                for cc in range(8):
                    b.emit("tensor",
                           lambda e, o=o, cc=cc, ph=ph, sa=sa: e.matmul(
                               ph[:, 0:256],
                               wb0.h[:, cc, 128 * o:128 * (o + 1)],
                               sa.h[:, cc, :],
                               start=(cc == 0), stop=(cc == 7)),
                           reads=[sa.rng(), wb0.rng()], writes=[phrng])
                b.emit("scalar", lambda e, o=o, ph=ph: e.activation(
                    scr2[:, 0:256], ph[:, 0:256], AF.Relu,
                    bias=nbqc[:, o:o + 1], scale=-1.0),
                    reads=[phrng, nbqc.rng()], writes=[scr2.rng()])
                b.emit("scalar", lambda e: e.activation(
                    scr2[:, 256:512], scr2[:, 0:256], AF.Exp,
                    bias=0.0, scale=-1.0),
                    reads=[scr2.rng()], writes=[scr2.rng()])
                b.emit("scalar", lambda e, o=o, ph=ph: e.activation(
                    scr2[:, 0:256], ph[:, 0:256], AF.Relu,
                    bias=bqc[:, o:o + 1], scale=1.0),
                    reads=[phrng, bqc.rng()], writes=[scr2.rng()])
                b.emit("vector", lambda e, o=o, sq=sq: e.tensor_tensor(
                    sq.h[:, o, :], scr2[:, 256:512], scr2[:, 0:256],
                    ALU.add),
                    reads=[scr2.rng()], writes=[sq.rng()])

        def p2_out(st):
            sq = stQ[st % 2]
            qdrng = ("ps", 3 * 4096, 3 * 4096 + 32)
            for i in range(2):
                t = 2 * st + i
                tsl = slice(128 * i, 128 * (i + 1))
                for hh in range(8):
                    b.emit("tensor",
                           lambda e, hh=hh, tsl=tsl, sq=sq: e.matmul(
                               pD[:, hh:hh + 1], sq.h[:, hh, tsl],
                               ksb.h[:, hh:hh + 1], start=True, stop=True),
                           reads=[sq.rng(), ksb.rng()], writes=[qdrng])
                b.emit("vector", lambda e: e.tensor_scalar(
                    zr[:], pD[:, 0:8], 1e-6, None, ALU.add),
                    reads=[qdrng], writes=[zr.rng()])
                b.emit("scalar", lambda e: e.activation(
                    zr[:], zr[:], AF.Ln, bias=0.0, scale=1.0),
                    reads=[zr.rng()], writes=[zr.rng()])
                b.emit("scalar", lambda e: e.activation(
                    zr[:], zr[:], AF.Exp, bias=0.0, scale=-1.0),
                    reads=[zr.rng()], writes=[zr.rng()])
                for hh in range(8):
                    b.emit("tensor",
                           lambda e, hh=hh, tsl=tsl, sq=sq: e.matmul(
                               pD[:, 128 * hh:128 * (hh + 1)],
                               sq.h[:, hh, tsl],
                               kvb.h[:, hh, :], start=True, stop=True),
                           reads=[sq.rng(), kvb.rng()], writes=[pDrng])
                b.emit("vector", lambda e: e.tensor_tensor(
                    ao[:].rearrange("p (a b) -> p a b", b=128),
                    pD[:].rearrange("p (a b) -> p a b", b=128),
                    zr[:].rearrange("p (a b) -> p a b", b=1)
                    .to_broadcast([128, 8, 128]),
                    ALU.mult),
                    reads=[pDrng, zr.rng()], writes=[ao.rng()])
                srng = stream.rng((NT * s + t) * 4096,
                                  (NT * s + t + 1) * 4096)
                m, r, lrng = ln_stats(c, scol, ao[:], ao.rng(),
                                      slot=c.lnslot())
                b.emit("vector", lambda e, m=m, r=r: e.tensor_scalar(
                    ao[:], ao[:], m, r, ALU.subtract, ALU.mult),
                    reads=[ao.rng(), lrng], writes=[ao.rng()])
                b.emit("vector", lambda e, s=s, t=t: e.tensor_tensor(
                    stream[:, NT * s + t, :], stream[:, NT * s + t, :],
                    ao[:], ALU.add),
                    reads=[srng, ao.rng()], writes=[srng])
        stagger(8, p2_regen, p2_qproj, p2_out)


EMITTERS["l0attn"] = emit_l0attn



def load_w_transposed(c, d, nm, wb, stg, tmb, identb):
    b, pool = c.b, c.pool
    pD, pDrng = pool.get(3)
    pbf = pD[:].bitcast(BF16)
    for kc in range(8):
        for hf in range(2):
            b.emit("sync", lambda e, nm=nm, kc=kc, hf=hf: e.dma_start(
                stg[:], d[nm][128 * kc:128 * (kc + 1),
                              512 * hf:512 * (hf + 1)]),
                reads=[("dr:" + nm, 0, 1)], writes=[stg.rng()], inc=16)
            b.emit("vector", lambda e, hf=hf: e.tensor_copy(
                tmb.h[:, 512 * hf:512 * (hf + 1)], stg[:]),
                reads=[stg.rng()], writes=[tmb.rng()])
        for cc in range(8):
            b.emit("tensor", lambda e, cc=cc, pbf=pbf:
                   e.transpose(pbf[:, 128 * cc:128 * (cc + 1)],
                               tmb.h[:, 128 * cc:128 * (cc + 1)], identb[:]),
                   reads=[tmb.rng(), identb.rng()], writes=[pDrng])
        b.emit("vector", lambda e, kc=kc, pbf=pbf: e.tensor_copy(
            wb.h[:, :, 128 * kc:128 * (kc + 1)],
            pbf[:, 0:1024].rearrange("p (a b) -> p a b", b=128)),
            reads=[pDrng], writes=[wb.rng()])


def emit_l0ffn(c, stream, d):
    b, ar, pool, nc = c.b, c.ar, c.pool, c.nc
    identb, onesb = c.identb, c.onesb
    scol = ar.shared("scol", [128, 64], F32)
    xn = ar.shared("xn", [128, 1024], BF16)
    stg = ar.shared("stg", [128, 512], F32)
    rbuf = ar.shared("facc", [128, 1024], F32)
    fout = ar.shared("fout", [128, 1024], F32)
    fs0 = ar.shared("fs0", [128, 512], F32)
    strip0 = ar.shared("strip0", [128, 8, 256], BF16)
    strip1 = ar.shared("strip1", [128, 8, 256], BF16)
    wb0 = ar.shared("wb0", [128, 8, 1024], BF16)
    wb1 = ar.shared("wb1", [128, 8, 1024], BF16)
    wb2 = ar.shared("wb2", [128, 10368], BF16)
    kvbb = ar.shared("kvb", [128, 8, 128], BF16)
    muc = ar.shared("cols4", [128, 8], F32)
    ktmb = Buf(strip1.h[:].rearrange("p a b -> p (a b)")[:, 1024:2048],
               "strip1", strip1.lo + 2048, strip1.lo + 4096)
    katile = Buf(kvbb.h, "kvb", kvbb.lo, kvbb.hi)
    bcols = ar.shared("bcols", [128, 32], BF16)

    rowf = Buf(stg.h, "stg", stg.lo, stg.hi)
    for (j, nm) in ((0, "L0_ffn_mu_r"), (2, "L0_ffn_mu_k")):
        vec_to_cols(c, d[nm], nm, muc, rowf, pool.get(0), n=1024, col0=0)
        b.emit("vector", lambda e, j=j: e.tensor_copy(
            bcols[:, 8 * j:8 * (j + 1)], muc.h[:, 0:8]),
            reads=[muc.rng()], writes=[bcols.rng()])
        b.emit("vector", lambda e: e.tensor_scalar(
            muc.h[:, 0:8], muc.h[:, 0:8], -1.0, 1.0, ALU.mult, ALU.add),
            reads=[muc.rng()], writes=[muc.rng()])
        b.emit("vector", lambda e, j=j: e.tensor_copy(
            bcols[:, 8 * (j + 1):8 * (j + 2)], muc.h[:, 0:8]),
            reads=[muc.rng()], writes=[bcols.rng()])

    load_w_transposed(c, d, "L0_ffn_W_r", wb0, stg, ktmb, identb)
    load_w_transposed(c, d, "L0_ffn_W_k", wb1, stg, ktmb, identb)
    wb2v = Buf(wb2.h[:, 0:8192].rearrange("p (a b) -> p a b", b=1024),
               "wb2", wb2.lo, wb2.lo + 16384)
    load_w_transposed(c, d, "L0_ffn_W_v", wb2v, stg, ktmb, identb)

    def mucol(j):
        return bcols.h[:, 8 * j:8 * (j + 1)].rearrange(
            "p (a b) -> p a b", b=1)

    pA, pArng = pool.get(0)
    pB, pBrng = pool.get(1)
    pC, pCrng = pool.get(2)
    tmp = Buf(xn.h[:].rearrange("p (a b) -> p a b", b=128), "xn", xn.lo, xn.hi)

    w2tail = 8192

    def mixbuf(t):
        if t % 2 == 0:
            xr = strip1.h[:, :, 0:128]
            xk = strip1.h[:, :, 128:256]
            rng = strip1.rng()
        else:
            f = wb2.h[:]
            xr = f[:, w2tail:w2tail + 1024].rearrange(
                "p (a b) -> p a b", b=128)
            xk = f[:, w2tail + 1024:w2tail + 2048].rearrange(
                "p (a b) -> p a b", b=128)
            rng = ("sb", wb2.lo + 2 * w2tail, wb2.lo + 2 * (w2tail + 2048))
        return xr, xk, rng

    for s in range(NSEQ):
        def stageA(t):
            srng = stream.rng((NT * s + t) * 4096, (NT * s + t + 1) * 4096)
            q = t % 2
            yT = strip0.h[:, :, 128 * q:128 * (q + 1)]
            yTp = strip0.h[:, :, 128 * (1 - q):128 * (2 - q)]
            m, r, lrng = ln_stats(c, scol, stream[:, NT * s + t, :], srng,
                                  slot=c.lnslot())
            b.emit("vector", lambda e, m=m, r=r, s=s, t=t: e.tensor_scalar(
                xn[:], stream[:, NT * s + t, :], m, r,
                ALU.subtract, ALU.mult),
                reads=[srng, lrng], writes=[xn.rng()])
            pbf = pC[:].bitcast(BF16)
            prng = ("ps", pCrng[1] + 2048 * q, pCrng[1] + 2048 * (q + 1))
            for cc in range(8):
                b.emit("tensor", lambda e, cc=cc, pbf=pbf, q=q:
                       e.transpose(pbf[:, 1024 * q + 128 * cc:
                                       1024 * q + 128 * (cc + 1)],
                                   xn[:, 128 * cc:128 * (cc + 1)],
                                   identb[:]),
                       reads=[xn.rng(), identb.rng()], writes=[prng])
            b.emit("vector", lambda e, q=q, pbf=pbf: e.tensor_copy(
                strip0.h[:, :, 128 * q:128 * (q + 1)],
                pbf[:, 1024 * q:1024 * (q + 1)]
                .rearrange("p (a b) -> p a b", b=128)),
                reads=[prng], writes=[strip0.rng()])
            xrT, xkT, mrng = mixbuf(t)
            for (xv, jmu) in ((xrT, 0), (xkT, 2)):
                b.emit("vector", lambda e, xv=xv, jmu=jmu, yT=yT:
                       e.tensor_tensor(
                           xv[:, :, 1:128], yT[:, :, 1:128],
                           mucol(jmu).to_broadcast([128, 8, 127]), ALU.mult),
                       reads=[strip0.rng(), bcols.rng()], writes=[mrng])
                b.emit("vector", lambda e, jmu=jmu, yT=yT:
                       e.tensor_tensor(
                           tmp.h[:, :, 0:127], yT[:, :, 0:127],
                           mucol(jmu + 1).to_broadcast([128, 8, 127]),
                           ALU.mult),
                       reads=[strip0.rng(), bcols.rng()], writes=[tmp.rng()])
                b.emit("vector", lambda e, xv=xv:
                       e.tensor_tensor(xv[:, :, 1:128], xv[:, :, 1:128],
                                       tmp.h[:, :, 0:127], ALU.add),
                       reads=[mrng, tmp.rng()], writes=[mrng])
                b.emit("vector", lambda e, xv=xv, jmu=jmu, yT=yT:
                       e.tensor_tensor(
                           xv[:, :, 0:1], yT[:, :, 0:1],
                           mucol(jmu).to_broadcast([128, 8, 1]), ALU.mult),
                       reads=[strip0.rng(), bcols.rng()], writes=[mrng])
                if t > 0:
                    b.emit("vector", lambda e, jmu=jmu, yTp=yTp:
                           e.tensor_tensor(
                               tmp.h[:, :, 127:128], yTp[:, :, 127:128],
                               mucol(jmu + 1).to_broadcast([128, 8, 1]),
                               ALU.mult),
                           reads=[strip0.rng(), bcols.rng()],
                           writes=[tmp.rng()])
                    b.emit("vector", lambda e, xv=xv:
                           e.tensor_tensor(xv[:, :, 0:1], xv[:, :, 0:1],
                                           tmp.h[:, :, 127:128], ALU.add),
                           reads=[mrng, tmp.rng()], writes=[mrng])

        def stageB(t):
            xrT, xkT, mrng = mixbuf(t)
            for h in range(2):
                for cc in range(8):
                    b.emit("tensor", lambda e, h=h, cc=cc, xrT=xrT: e.matmul(
                        pA[:, 512 * h:512 * (h + 1)],
                        xrT[:, cc, :],
                        wb0.h[:, cc, 512 * h:512 * (h + 1)],
                        start=(cc == 0), stop=(cc == 7)),
                        reads=[mrng, wb0.rng()], writes=[pArng])
            b.emit("scalar", lambda e: e.activation(
                rbuf[:], pA[:], AF.Sigmoid, bias=0.0, scale=1.0),
                reads=[pArng], writes=[rbuf.rng()])
            for o in range(8):
                for cc in range(8):
                    b.emit("tensor", lambda e, o=o, cc=cc, xkT=xkT: e.matmul(
                        pB[:, 128 * o:128 * (o + 1)],
                        wb1.h[:, cc, 128 * o:128 * (o + 1)],
                        xkT[:, cc, :],
                        start=(cc == 0), stop=(cc == 7)),
                        reads=[mrng, wb1.rng()], writes=[pBrng])
                b.emit("scalar", lambda e, o=o: e.activation(
                    fs0[:, 0:128], pB[:, 128 * o:128 * (o + 1)], AF.Relu,
                    bias=0.0, scale=1.0),
                    reads=[pBrng], writes=[fs0.rng()])
                b.emit("scalar", lambda e, o=o: e.activation(
                    katile.h[:, o, :], fs0[:, 0:128], AF.Square,
                    bias=0.0, scale=1.0),
                    reads=[fs0.rng()], writes=[katile.rng()])

        def stageC(t):
            srng = stream.rng((NT * s + t) * 4096, (NT * s + t + 1) * 4096)
            pD2, pD2rng = pool.get(3)
            for h in range(2):
                for dc in range(8):
                    b.emit("tensor", lambda e, h=h, dc=dc, pD2=pD2: e.matmul(
                        pD2[:, 512 * h:512 * (h + 1)],
                        katile.h[:, dc, :],
                        wb2v.h[:, dc, 512 * h:512 * (h + 1)],
                        start=(dc == 0), stop=(dc == 7)),
                        reads=[katile.rng(), wb2v.rng()], writes=[pD2rng])
            b.emit("vector", lambda e, pD2=pD2: e.tensor_tensor(
                fout[:], rbuf[:], pD2[:], ALU.mult),
                reads=[rbuf.rng(), pD2rng], writes=[fout.rng()])
            m, r, lrng = ln_stats(c, scol, fout[:], fout.rng(),
                                  slot=c.lnslot())
            b.emit("vector", lambda e, m=m, r=r: e.tensor_scalar(
                fout[:], fout[:], m, r, ALU.subtract, ALU.mult),
                reads=[fout.rng(), lrng], writes=[fout.rng()])
            b.emit("vector", lambda e, s=s, t=t: e.tensor_tensor(
                stream[:, NT * s + t, :], stream[:, NT * s + t, :],
                fout[:], ALU.add),
                reads=[srng, fout.rng()], writes=[srng])
        stagger(NT, stageA, stageB, stageC)


EMITTERS["l0ffn"] = emit_l0ffn



def emit_l1attn(c, stream, d):
    b, ar, pool, nc = c.b, c.ar, c.pool, c.nc
    identb, onesb, onesf, maskb = c.identb, c.onesb, c.onesf, c.maskb
    mstats = ar.shared("mstats", [128, 32], F32)
    scol = ar.shared("scol", [128, 64], F32)
    xn = ar.shared("xn", [128, 1024], BF16)
    stg = ar.shared("stg", [128, 512], F32)
    facc = ar.shared("facc", [128, 1024], F32)
    fout = ar.shared("fout", [128, 1024], F32)
    fs0 = ar.shared("fs0", [128, 512], F32)
    kvb = ar.shared("kvb", [128, 8, 128], BF16)
    strip0 = ar.shared("strip0", [128, 8, 256], BF16)
    strip1 = ar.shared("strip1", [128, 8, 256], BF16)
    wb0 = ar.shared("wb0", [128, 8, 1024], BF16)
    wb1 = ar.shared("wb1", [128, 8, 1024], BF16)
    wb2 = ar.shared("wb2", [128, 10368], BF16)
    bcols = ar.shared("bcols", [128, 32], BF16)
    colsA = ar.shared("colsA", [128, 16], F32)
    colsB = ar.shared("colsB", [128, 16], F32)
    colsC = ar.shared("colsC", [128, 8], F32)
    wgc = ar.shared("cols4", [128, 8], F32)
    rowf = Buf(stg.h, "stg", stg.lo, stg.hi)

    s0f = strip0.h[:].rearrange("p a b -> p (a b)")
    s1f = strip1.h[:].rearrange("p a b -> p (a b)")

    def s0r(lo, hi):
        return ("sb", strip0.lo + 2 * lo, strip0.lo + 2 * hi)

    def s1r(lo, hi):
        return ("sb", strip1.lo + 2 * lo, strip1.lo + 2 * hi)

    def w0r(lo, hi):
        return ("sb", wb0.lo + 2 * lo, wb0.lo + 2 * hi)

    def w2r(lo, hi):
        return ("sb", wb2.lo + 2 * lo, wb2.lo + 2 * hi)
    w0f = wb0.h[:].rearrange("p a b -> p (a b)")
    w1f = wb1.h[:].rearrange("p a b -> p (a b)")
    w2f = wb2.h[:]
    W2RNG = wb2.rng()
    S0RNG, S1RNG = strip0.rng(), strip1.rng()
    W0RNG, W1RNG = wb0.rng(), wb1.rng()
    CGOFF = 8320

    pA, pArng = pool.get(0)
    pB, pBrng = pool.get(1)
    pC, pCrng = pool.get(2)
    pD, pDrng = pool.get(3)

    vec_to_cols(c, d["L1_attn_b_lqkv"], "b_lqkv", colsA, rowf,
                pool.get(0), n=1536, col0=0)
    vec_to_cols(c, d["L1_attn_b_comp"], "b_comp", colsA, rowf,
                pool.get(0), n=1024, col0=8)
    bcomp_c = colsA.h[:, 12:16]
    vec_to_cols(c, d["L1_attn_b_gqkv"], "b_gqkv", colsB, rowf,
                pool.get(0), n=1536, col0=0)
    vec_to_cols(c, d["L1_attn_b_lproj"], "b_lproj", colsB, rowf,
                pool.get(0), n=512, col0=12)
    vec_to_cols(c, d["L1_attn_b_gproj"], "b_gproj", colsC, rowf,
                pool.get(0), n=512, col0=0)
    b.emit("sync", lambda e: e.dma_start(
        stg[:, 0:16].rearrange("p (a b) -> p a b", b=2),
        d["L1_attn_w_gate"][:].rearrange("(k p) n -> p k n", p=128)),
        reads=[("dr:L1_attn_w_gate", 0, 1)], writes=[stg.rng()], inc=16)
    b.emit("vector", lambda e: e.tensor_tensor(
        wgc.h[:, 0:8].rearrange("p (a b) -> p a b", b=1),
        stg[:, 0:16].rearrange("p (a b) -> p a b", b=2)[:, :, 0:1],
        stg[:, 0:16].rearrange("p (a b) -> p a b", b=2)[:, :, 1:2],
        ALU.subtract),
        reads=[stg.rng()], writes=[wgc.rng()])
    b.emit("vector", lambda e: e.tensor_copy(bcols.h[:, 16:24],
                                             wgc.h[:, 0:8]),
           reads=[wgc.rng()], writes=[bcols.rng()])
    b.emit("sync", lambda e: e.dma_start(
        stg[0:1, 0:2], d["L1_attn_b_gate"][:].rearrange("(a n) -> a n", a=1)),
        reads=[("dr:L1_attn_b_gate", 0, 1)], writes=[stg.rng()], inc=16)
    b.emit("vector", lambda e: e.tensor_tensor(
        stg[0:1, 2:3], stg[0:1, 0:1], stg[0:1, 1:2], ALU.subtract),
        reads=[stg.rng()], writes=[stg.rng()])
    b.emit("tensor", lambda e: e.matmul(
        pC[:, 0:1], onesf[0:1, 0:128], stg[0:1, 2:3], start=True, stop=True),
        reads=[stg.rng(), onesf.rng()], writes=[pCrng])
    b.emit("vector", lambda e: e.tensor_copy(colsC.h[:, 4:5], pC[:, 0:1]),
           reads=[pCrng], writes=[colsC.rng()])
    b.emit("vector", lambda e: e.tensor_scalar_mul(colsC.h[:, 5:6],
                                                   colsC.h[:, 4:5], -1.0),
           reads=[colsC.rng()], writes=[colsC.rng()])

    rrslot = [0]

    def regen_seq(s, items):
        infos = []
        for (t, cc, dst_ap, dst_rng) in items:
            sl = rrslot[0] % 8
            psl = rrslot[0] % 2
            rrslot[0] += 1
            infos.append((t, cc, dst_ap, dst_rng, sl, psl))
        pbf = pC[:].bitcast(BF16)

        def st0(i):
            (t, cc, dst_ap, dst_rng, sl, psl) = infos[i]
            srng = stream.rng((NT * s + t) * 4096, (NT * s + t + 1) * 4096)
            mc = mstats[:, 2 * t:2 * t + 1]
            rc = mstats[:, 2 * t + 1:2 * t + 2]
            xr = ("sb", xn.lo + 256 * sl, xn.lo + 256 * (sl + 1))
            pr = ("ps", pCrng[1] + 2048 * psl, pCrng[1] + 2048 * (psl + 1))
            b.emit("vector", lambda e, t=t, cc=cc, mc=mc, rc=rc, sl=sl:
                   e.tensor_scalar(
                       xn[:, 128 * sl:128 * (sl + 1)],
                       stream[:, NT * s + t, 128 * cc:128 * (cc + 1)],
                       mc, rc, ALU.subtract, ALU.mult),
                   reads=[srng, mstats.rng()], writes=[xr])
            b.emit("tensor", lambda e, sl=sl, psl=psl: e.transpose(
                pbf[:, 1024 * psl:1024 * psl + 128],
                xn[:, 128 * sl:128 * (sl + 1)], identb[:]),
                reads=[xr, identb.rng()], writes=[pr])

        def st1(i):
            (t, cc, dst_ap, dst_rng, sl, psl) = infos[i]
            pr = ("ps", pCrng[1] + 2048 * psl, pCrng[1] + 2048 * (psl + 1))
            b.emit("vector", lambda e, dst_ap=dst_ap, psl=psl:
                   e.tensor_copy(dst_ap,
                                 pbf[:, 1024 * psl:1024 * psl + 128]),
                   reads=[pr], writes=[dst_rng])
        stagger(len(infos), st0, st1)

    def regen_local_strip(s, st):
        regen_seq(s, [(2 * st + i, cc,
                       s0f[:, 256 * cc + 128 * i:256 * cc + 128 * (i + 1)],
                       s0r(256 * cc + 128 * i, 256 * cc + 128 * (i + 1)))
                      for i in range(2) for cc in range(4)])

    def load_512(nm, dst_ap, dst_rng, rows, cols):
        b.emit("sync", lambda e: e.dma_start(
            stg[:, 0:512], d[nm][rows, cols]),
            reads=[("dr:" + nm, 0, 1)], writes=[stg.rng()], inc=16)
        b.emit("vector", lambda e: e.tensor_copy(dst_ap, stg[:, 0:512]),
               reads=[stg.rng()], writes=[dst_rng])

    def attn_head_sbuf(h, nkt, qlen, lkT, lkT_rng, ext0, out_dst, out_rng,
                       pscr, pscr_rng, lqT, q0):
        for kt in range(nkt):
            qoff = max(0, 128 * kt - q0)
            qw = qlen - qoff
            b.emit("tensor", lambda e, kt=kt, qoff=qoff, qw=qw: e.matmul(
                pA[:, 0:qw], lkT[:, 128 * kt:128 * (kt + 1)],
                lqT[:, qoff:qoff + qw], start=True, stop=True),
                reads=[lkT_rng, S0RNG], writes=[pArng])
            b.emit("scalar", lambda e, qw=qw: e.activation(
                pscr[:, 0:qw], pA[:, 0:qw], AF.Exp, bias=0.0, scale=0.125),
                reads=[pArng], writes=[pscr_rng])
            if 128 * kt >= q0:
                b.emit("vector", lambda e: e.tensor_tensor(
                    pscr[:, 0:128], pscr[:, 0:128], maskb[:], ALU.mult),
                    reads=[pscr_rng, maskb.rng()], writes=[pscr_rng])
            b.emit("tensor", lambda e, kt=kt, qw=qw: e.matmul(
                pB[0:65, 0:qw],
                w2f[:, ext0 + 520 * kt + 65 * h:ext0 + 520 * kt + 65 * (h + 1)],
                pscr[:, 0:qw], start=True, stop=True),
                reads=[W2RNG, pscr_rng], writes=[pBrng])
            if kt == 0:
                b.emit("vector", lambda e, qw=qw: e.tensor_copy(
                    facc[0:65, 0:qw], pB[0:65, 0:qw]),
                    reads=[pBrng], writes=[facc.rng()])
            else:
                b.emit("vector", lambda e, qoff=qoff, qw=qw: e.tensor_tensor(
                    facc[0:65, qoff:qoff + qw], facc[0:65, qoff:qoff + qw],
                    pB[0:65, 0:qw], ALU.add),
                    reads=[facc.rng(), pBrng], writes=[facc.rng()])
        b.emit("scalar", lambda e: e.activation(
            facc[64:65, 0:qlen], facc[64:65, 0:qlen], AF.Ln,
            bias=0.0, scale=1.0),
            reads=[facc.rng()], writes=[facc.rng()])
        b.emit("scalar", lambda e: e.activation(
            facc[64:65, 0:qlen], facc[64:65, 0:qlen], AF.Exp,
            bias=0.0, scale=-1.0),
            reads=[facc.rng()], writes=[facc.rng()])
        b.emit("tensor", lambda e: e.matmul(
            pC[0:64, 0:qlen], onesf[64:65, 0:64], facc[64:65, 0:qlen],
            start=True, stop=True),
            reads=[facc.rng(), onesf.rng()], writes=[pCrng])
        b.emit("vector", lambda e: e.tensor_tensor(
            out_dst, facc[0:64, 0:qlen], pC[0:64, 0:qlen], ALU.mult),
            reads=[facc.rng(), pCrng], writes=[out_rng])

    def attn_head(h, nkt, qlen, lkT, lkT_rng, ext0, out_dst, out_rng,
                  pscr, pscr_rng, lqT):
        q0 = 128 * nkt - qlen
        if not PV_PSUM:
            return attn_head_sbuf(h, nkt, qlen, lkT, lkT_rng, ext0, out_dst,
                                  out_rng, pscr, pscr_rng, lqT, q0)
        hb = h % 2
        pvrng = ("ps", pBrng[1] + 2048 * hb, pBrng[1] + 2048 * (hb + 1))
        pv = pB[:, 512 * hb:512 * (hb + 1)]
        pw = 256 if qlen <= 256 else 512
        nslots = 1024 // pw

        def _geo(kt):
            sb = kt % 2
            psl = kt % nslots
            return (sb,
                    ("ps", pArng[1] + 2048 * sb, pArng[1] + 2048 * (sb + 1)),
                    ("sb", pscr_rng[1] + 2 * pw * psl,
                     pscr_rng[1] + 2 * pw * (psl + 1)),
                    pscr[:, pw * psl:pw * psl + qlen],
                    max(0, 128 * kt - q0))

        def st_s(kt):
            sb, srng_, prng_, ps, qoff = _geo(kt)
            qw = qlen - qoff
            b.emit("tensor", lambda e, kt=kt, qoff=qoff, qw=qw, sb=sb:
                   e.matmul(
                       pA[:, 512 * sb:512 * sb + qw],
                       lkT[:, 128 * kt:128 * (kt + 1)],
                       lqT[:, qoff:qoff + qw],
                       start=True, stop=True),
                   reads=[lkT_rng, S0RNG], writes=[srng_])

        def st_e(kt):
            sb, srng_, prng_, ps, qoff = _geo(kt)
            qw = qlen - qoff
            b.emit("scalar", lambda e, qw=qw, sb=sb, ps=ps: e.activation(
                ps[:, 0:qw], pA[:, 512 * sb:512 * sb + qw], AF.Exp,
                bias=0.0, scale=0.125),
                reads=[srng_], writes=[prng_])
            if 128 * kt >= q0:
                b.emit("vector", lambda e, ps=ps: e.tensor_tensor(
                    ps[:, 0:128], ps[:, 0:128], maskb[:], ALU.mult),
                    reads=[prng_, maskb.rng()], writes=[prng_])

        def st_pv(kt):
            sb, srng_, prng_, ps, qoff = _geo(kt)
            qw = qlen - qoff
            b.emit("tensor", lambda e, kt=kt, qw=qw, qoff=qoff, ps=ps:
                   e.matmul(
                       pv[0:65, qoff:qoff + qw],
                       w2f[:, ext0 + 520 * kt + 65 * h:
                           ext0 + 520 * kt + 65 * (h + 1)],
                       ps[:, 0:qw],
                       start=(kt == 0), stop=(kt == nkt - 1)),
                   reads=[W2RNG, prng_], writes=[pvrng])
        stagger(nkt, st_s, st_e, st_pv)
        fsl = 512 * hb
        frng = ("sb", facc.lo + 4 * fsl, facc.lo + 4 * (fsl + qlen))
        b.emit("scalar", lambda e: e.activation(
            facc[64:65, fsl:fsl + qlen], pv[64:65, 0:qlen], AF.Ln,
            bias=0.0, scale=1.0),
            reads=[pvrng], writes=[frng])
        b.emit("scalar", lambda e: e.activation(
            facc[64:65, fsl:fsl + qlen], facc[64:65, fsl:fsl + qlen],
            AF.Exp, bias=0.0, scale=-1.0),
            reads=[frng], writes=[frng])
        b.emit("tensor", lambda e: e.matmul(
            pC[0:64, 0:qlen], onesf[64:65, 0:64],
            facc[64:65, fsl:fsl + qlen],
            start=True, stop=True),
            reads=[frng, onesf.rng()], writes=[pCrng])
        b.emit("vector", lambda e: e.tensor_copy(
            facc[0:64, fsl:fsl + qlen], pC[0:64, 0:qlen]),
            reads=[pCrng], writes=[frng])
        b.emit("vector", lambda e: e.tensor_tensor(
            out_dst, pv[0:64, 0:qlen], facc[0:64, fsl:fsl + qlen], ALU.mult),
            reads=[pvrng, frng], writes=[out_rng])

    pscr = fs0.h[:].bitcast(BF16)
    pscr_rng = fs0.rng()

    for s in range(NSEQ):
        for t in range(NT):
            srng = stream.rng((NT * s + t) * 4096, (NT * s + t + 1) * 4096)
            m, r, lrng = ln_stats(c, scol, stream[:, NT * s + t, :], srng,
                                  slot=c.lnslot())
            b.emit("vector", lambda e, m=m, t=t: e.tensor_copy(
                mstats[:, 2 * t:2 * t + 1], m),
                reads=[lrng], writes=[mstats.rng()])
            b.emit("vector", lambda e, r=r, t=t: e.tensor_copy(
                mstats[:, 2 * t + 1:2 * t + 2], r),
                reads=[lrng], writes=[mstats.rng()])

        kvf = kvb.h[:].rearrange("p a b -> p (a b)")
        obank = [("ps", pArng[1], pArng[1] + 2048),
                 ("ps", pArng[1] + 2048, pArng[1] + 4096),
                 ("ps", pBrng[1], pBrng[1] + 2048),
                 ("ps", pBrng[1] + 2048, pBrng[1] + 4096)]
        for cc in range(8):
            regen_seq(s, [(t, cc, s0f[:, 128 * t:128 * (t + 1)],
                           s0r(128 * t, 128 * (t + 1))) for t in range(NT)])
            for r4 in range(4):
                kci = r4 * 8 + cc
                ks = r4 % 2
                krng = ("sb", kvb.lo + 1024 * ks, kvb.lo + 1024 * (ks + 1))
                b.emit("sync", lambda e, kci=kci: e.dma_start(
                    stg[:, 0:512],
                    d["L1_attn_w_comp"][128 * kci:128 * (kci + 1), 512:1024]),
                    reads=[("dr:L1_attn_w_comp", 0, 1)], writes=[stg.rng()],
                    inc=16)
                b.emit("vector", lambda e, ks=ks: e.tensor_copy(
                    kvf[:, 512 * ks:512 * (ks + 1)], stg[:, 0:512]),
                    reads=[stg.rng()], writes=[krng])
                for o in range(4):
                    px = pA if o < 2 else pB
                    b.emit("tensor", lambda e, o=o, r4=r4, px=px, ks=ks:
                           e.matmul(
                               px[:, 512 * (o % 2):512 * (o % 2) + 512],
                               kvf[:, 512 * ks + 128 * o:
                                   512 * ks + 128 * (o + 1)],
                               s0f[:, r4:2048:4],
                               start=(r4 == 0), stop=(r4 == 3)),
                           reads=[krng, S0RNG], writes=[obank[o]])
            for (acc, px, r0, r1) in ((facc, pA, 0, 2), (fout, pB, 2, 4)):
                if cc == 0:
                    b.emit("vector", lambda e, acc=acc, px=px:
                           e.tensor_copy(acc[:], px[:]),
                           reads=[obank[r0], obank[r1 - 1]],
                           writes=[acc.rng()])
                else:
                    b.emit("vector", lambda e, acc=acc, px=px:
                           e.tensor_tensor(acc[:], acc[:], px[:], ALU.add),
                           reads=[acc.rng(), obank[r0], obank[r1 - 1]],
                           writes=[acc.rng()])
        for o in range(4):
            acc = facc if o < 2 else fout
            b.emit("scalar", lambda e, o=o, acc=acc: e.activation(
                w2f[:, CGOFF + 512 * o:CGOFF + 512 * (o + 1)],
                acc[:, 512 * (o % 2):512 * (o % 2 + 1)],
                AF.Identity, bias=bcomp_c[:, o:o + 1], scale=1.0),
                reads=[acc.rng(), colsA.rng()], writes=[W2RNG])

        for cc4 in range(4):
            load_512("L1_attn_w_gqkv",
                     w1f[:, 1536 * cc4:1536 * cc4 + 512], W1RNG,
                     slice(128 * cc4, 128 * (cc4 + 1)), slice(0, 512))
            load_512("L1_attn_w_gqkv",
                     w1f[:, 1536 * cc4 + 512:1536 * cc4 + 1024], W1RNG,
                     slice(128 * cc4, 128 * (cc4 + 1)), slice(512, 1024))
            load_512("L1_attn_w_gqkv",
                     w1f[:, 1536 * cc4 + 1024:1536 * (cc4 + 1)], W1RNG,
                     slice(128 * cc4, 128 * (cc4 + 1)), slice(1024, 1536))
        for o in range(12):
            for cc4 in range(4):
                b.emit("tensor", lambda e, o=o, cc4=cc4: e.matmul(
                    pA[:, 0:512],
                    w1f[:, 1536 * cc4 + 128 * o:1536 * cc4 + 128 * (o + 1)],
                    w2f[:, CGOFF + 512 * cc4:CGOFF + 512 * (cc4 + 1)],
                    start=(cc4 == 0), stop=(cc4 == 3)),
                    reads=[W1RNG, W2RNG], writes=[pArng])
            if o < 4:
                dst, drng = s0f[:, 512 * o:512 * (o + 1)], S0RNG
            elif o < 8:
                dst, drng = s1f[:, 512 * (o - 4):512 * (o - 3)], S1RNG
            else:
                dst, drng = w0f[:, 512 * (o - 8):512 * (o - 7)], W0RNG
            b.emit("scalar", lambda e, o=o, dst=dst: e.activation(
                dst, pA[:, 0:512], AF.Identity, bias=colsB.h[:, o:o + 1],
                scale=1.0),
                reads=[pArng, colsB.rng()], writes=[drng])
        for kt in range(4):
            b.emit("vector", lambda e, kt=kt: e.memset(
                w2f[:, 520 * kt:520 * (kt + 1)]
                .rearrange("p (a b) -> p a b", b=65)[:, :, 64:65], 1.0),
                writes=[w2r(520 * kt, 520 * (kt + 1))])
        pbf = pC[:].bitcast(BF16)
        for vch in range(4):
            for kt in range(4):
                b.emit("tensor", lambda e, vch=vch, kt=kt, pbf=pbf:
                       e.transpose(pbf[:, 0:128],
                                   w0f[:, 512 * vch + 128 * kt:
                                       512 * vch + 128 * (kt + 1)],
                                   identb[:]),
                       reads=[W0RNG, identb.rng()], writes=[pCrng])
                b.emit("vector", lambda e, vch=vch, kt=kt, pbf=pbf:
                       e.tensor_copy(
                           w2f[:, 520 * kt + 130 * vch:
                               520 * kt + 130 * (vch + 1)]
                           .rearrange("p (a b) -> p a b", b=65)[:, :, 0:64],
                           pbf[:, 0:128].rearrange("p (a b) -> p a b", b=64)),
                       reads=[pCrng],
                       writes=[w2r(520 * kt + 130 * vch,
                                   520 * kt + 130 * (vch + 1))])

        for h in range(8):
            row = 64 * (h % 2)
            ch = h // 2
            lqT = s0f[:, 512 * ch:512 * (ch + 1)][row:row + 64, :]
            lkT = s1f[:, 512 * ch:512 * (ch + 1)][row:row + 64, :]
            out_dst = s0f[:, 512 * ch:512 * (ch + 1)][row:row + 64, :]
            attn_head(h, 4, 512, lkT, S1RNG, 0, out_dst,
                      s0r(512 * ch, 512 * (ch + 1)),
                      pscr, pscr_rng, lqT)

        for cc4 in range(4):
            load_512("L1_attn_w_gproj",
                     w1f[:, 6144 + 512 * cc4:6144 + 512 * (cc4 + 1)], W1RNG,
                     slice(128 * cc4, 128 * (cc4 + 1)), slice(0, 512))
        for o in range(4):
            for cc4 in range(4):
                b.emit("tensor", lambda e, o=o, cc4=cc4: e.matmul(
                    pA[:, 0:512],
                    w1f[:, 6144 + 512 * cc4 + 128 * o:
                        6144 + 512 * cc4 + 128 * (o + 1)],
                    s0f[:, 512 * cc4:512 * (cc4 + 1)],
                    start=(cc4 == 0), stop=(cc4 == 3)),
                    reads=[W1RNG, S0RNG], writes=[pArng])
            b.emit("scalar", lambda e, o=o: e.activation(
                w2f[:, CGOFF + 512 * o:CGOFF + 512 * (o + 1)], pA[:, 0:512],
                AF.Identity, bias=colsC.h[:, o:o + 1], scale=1.0),
                reads=[pArng, colsC.rng()], writes=[W2RNG])

        for cc4 in range(4):
            load_512("L1_attn_w_lqkv",
                     w1f[:, 1536 * cc4:1536 * cc4 + 512], W1RNG,
                     slice(128 * cc4, 128 * (cc4 + 1)), slice(0, 512))
            load_512("L1_attn_w_lqkv",
                     w1f[:, 1536 * cc4 + 512:1536 * cc4 + 1024], W1RNG,
                     slice(128 * cc4, 128 * (cc4 + 1)), slice(512, 1024))
            load_512("L1_attn_w_lqkv",
                     w1f[:, 1536 * cc4 + 1024:1536 * (cc4 + 1)], W1RNG,
                     slice(128 * cc4, 128 * (cc4 + 1)), slice(1024, 1536))
        for kt in range(16):
            b.emit("vector", lambda e, kt=kt: e.memset(
                w2f[:, 520 * kt:520 * (kt + 1)]
                .rearrange("p (a b) -> p a b", b=65)[:, :, 64:65], 1.0),
                writes=[w2r(520 * kt, 520 * (kt + 1))])
        b.emit("sync", lambda e: e.dma_start(
            stg[0:1, 0:512],
            d["L1_attn_b_lqkv"][:].rearrange("(a n) -> a n", a=1)
            [:, 1024:1536]),
            reads=[("dr:b_lqkv", 0, 1)], writes=[stg.rng()], inc=16)
        b.emit("tensor", lambda e: e.matmul(
            pD[:, 0:512], onesf[0:1, 0:128], stg[0:1, 0:512],
            start=True, stop=True),
            reads=[stg.rng(), onesf.rng()], writes=[pDrng])
        b.emit("vector", lambda e: e.tensor_copy(fs0[:], pD[:, 0:512]),
               reads=[pDrng], writes=[fs0.rng()])
        for st in range(8):
            regen_local_strip(s, st)
            for o in range(4):
                for cc4 in range(4):
                    b.emit("tensor", lambda e, o=o, cc4=cc4: e.matmul(
                        pA[:, 0:256],
                        w1f[:, 1536 * cc4 + 512 + 128 * o:
                            1536 * cc4 + 512 + 128 * (o + 1)],
                        s0f[:, 256 * cc4:256 * (cc4 + 1)],
                        start=(cc4 == 0), stop=(cc4 == 3)),
                        reads=[W1RNG, S0RNG], writes=[pArng])
                b.emit("scalar", lambda e, o=o, st=st: e.activation(
                    w0f[:, 2048 * o + 256 * st:2048 * o + 256 * (st + 1)],
                    pA[:, 0:256], AF.Identity, bias=colsB.h[:, 4 + o:5 + o],
                    scale=1.0),
                    reads=[pArng, colsB.rng()],
                    writes=[w0r(2048 * o + 256 * st,
                                2048 * o + 256 * (st + 1))])
            for i in range(2):
                kt = 2 * st + i
                for cc4 in range(4):
                    b.emit("tensor", lambda e, i=i, cc4=cc4: e.matmul(
                        pB[:, 0:512],
                        s0f[:, 256 * cc4 + 128 * i:256 * cc4 + 128 * (i + 1)],
                        w1f[:, 1536 * cc4 + 1024:1536 * (cc4 + 1)],
                        start=(cc4 == 0), stop=(cc4 == 3)),
                        reads=[W1RNG, S0RNG], writes=[pBrng])
                b.emit("vector", lambda e, kt=kt: e.scalar_tensor_tensor(
                    w2f[:, 520 * kt:520 * (kt + 1)]
                    .rearrange("p (a b) -> p a b", b=65)[:, :, 0:64],
                    pB[:, 0:512].rearrange("p (a b) -> p a b", b=64),
                    1.0, fs0.h[:].rearrange("p (a b) -> p a b", b=64),
                    ALU.mult, ALU.add),
                    reads=[pBrng, fs0.rng()],
                    writes=[w2r(520 * kt, 520 * (kt + 1))])

        for cc4 in range(4):
            load_512("L1_attn_w_lproj",
                     w1f[:, 6144 + 512 * cc4:6144 + 512 * (cc4 + 1)], W1RNG,
                     slice(128 * cc4, 128 * (cc4 + 1)), slice(0, 512))
        kvf = kvb.h[:].rearrange("p a b -> p (a b)")
        for st in range(8):
            regen_local_strip(s, st)
            for o in range(4):
                for cc4 in range(4):
                    b.emit("tensor", lambda e, o=o, cc4=cc4: e.matmul(
                        pA[:, 0:256],
                        w1f[:, 1536 * cc4 + 128 * o:
                            1536 * cc4 + 128 * (o + 1)],
                        s0f[:, 256 * cc4:256 * (cc4 + 1)],
                        start=(cc4 == 0), stop=(cc4 == 3)),
                        reads=[W1RNG, S0RNG], writes=[pArng])
                b.emit("scalar", lambda e, o=o: e.activation(
                    s0f[:, 1024 + 256 * o:1024 + 256 * (o + 1)],
                    pA[:, 0:256], AF.Identity, bias=colsA.h[:, o:o + 1],
                    scale=1.0),
                    reads=[pArng, colsA.rng()],
                    writes=[s0r(1024 + 256 * o, 1024 + 256 * (o + 1))])
            for h in range(8):
                row = 64 * (h % 2)
                ch = h // 2
                lqT = s0f[:, 1024 + 256 * ch:1024 + 256 * (ch + 1)][
                    row:row + 64, :]
                lkT = w0f[:, 2048 * ch:2048 * (ch + 1)][row:row + 64, :]
                out_dst = s1f[:, 256 * ch:256 * (ch + 1)][row:row + 64, :]
                attn_head(h, 2 * st + 2, 256, lkT, W0RNG, 0, out_dst,
                          s1r(256 * ch, 256 * (ch + 1)),
                          pscr, pscr_rng, lqT)
            for o in range(4):
                for cc4 in range(4):
                    b.emit("tensor", lambda e, o=o, cc4=cc4: e.matmul(
                        pA[:, 0:256],
                        w1f[:, 6144 + 512 * cc4 + 128 * o:
                            6144 + 512 * cc4 + 128 * (o + 1)],
                        s1f[:, 256 * cc4:256 * (cc4 + 1)],
                        start=(cc4 == 0), stop=(cc4 == 3)),
                        reads=[W1RNG, S1RNG], writes=[pArng])
                b.emit("scalar", lambda e, o=o: e.activation(
                    s1f[:, 1024 + 256 * o:1024 + 256 * (o + 1)],
                    pA[:, 0:256], AF.Identity, bias=colsB.h[:, 12 + o:13 + o],
                    scale=1.0),
                    reads=[pArng, colsB.rng()],
                    writes=[s1r(1024 + 256 * o, 1024 + 256 * (o + 1))])
            for i in range(2):
                t = 2 * st + i
                srng = stream.rng((NT * s + t) * 4096,
                                  (NT * s + t + 1) * 4096)
                for cc4 in range(4):
                    b.emit("vector", lambda e, cc4=cc4, t=t: e.tensor_copy(
                        kvf[:, 128 * cc4:128 * (cc4 + 1)]
                        .rearrange("p (a b) -> p a b", b=4),
                        w2f[:, CGOFF + 512 * cc4 + 32 * t:
                            CGOFF + 512 * cc4 + 32 * (t + 1)]
                        .rearrange("p (a b) -> p a b", b=1)
                        .to_broadcast([128, 32, 4])),
                        reads=[W2RNG], writes=[kvb.rng()])
                for cc in range(8):
                    if cc < 4:
                        lh = s1f[:, 1024 + 256 * cc + 128 * i:
                                 1024 + 256 * cc + 128 * (i + 1)]
                        lrng = S1RNG
                    else:
                        lh = kvf[:, 128 * (cc - 4):128 * (cc - 3)]
                        lrng = kvb.rng()
                    b.emit("tensor", lambda e, cc=cc, lh=lh, lrng=lrng:
                           e.matmul(
                               pC[:, 0:1], lh, bcols.h[:, 16 + cc:17 + cc],
                               start=(cc == 0), stop=(cc == 7)),
                           reads=[lrng, bcols.rng()], writes=[pCrng])
                b.emit("scalar", lambda e: e.activation(
                    colsC.h[:, 6:7], pC[:, 0:1], AF.Exp,
                    bias=colsC.h[:, 5:6], scale=-1.0),
                    reads=[pCrng, colsC.rng()], writes=[colsC.rng()])
                b.emit("vector", lambda e: e.tensor_scalar(
                    colsC.h[:, 6:7], colsC.h[:, 6:7], 1.0, None, ALU.add),
                    reads=[colsC.rng()], writes=[colsC.rng()])
                b.emit("scalar", lambda e: e.activation(
                    colsC.h[:, 6:7], colsC.h[:, 6:7], AF.Ln,
                    bias=0.0, scale=1.0),
                    reads=[colsC.rng()], writes=[colsC.rng()])
                b.emit("scalar", lambda e: e.activation(
                    colsC.h[:, 6:7], colsC.h[:, 6:7], AF.Exp,
                    bias=0.0, scale=-1.0),
                    reads=[colsC.rng()], writes=[colsC.rng()])
                b.emit("vector", lambda e: e.tensor_scalar(
                    colsC.h[:, 7:8], colsC.h[:, 6:7], -1.0, 1.0,
                    ALU.mult, ALU.add),
                    reads=[colsC.rng()], writes=[colsC.rng()])
                pbf2 = pD[:].bitcast(BF16)
                for cc4 in range(4):
                    b.emit("tensor", lambda e, cc4=cc4, i=i, pbf2=pbf2:
                           e.transpose(
                               pbf2[:, 128 * cc4:128 * (cc4 + 1)],
                               s1f[:, 1024 + 256 * cc4 + 128 * i:
                                   1024 + 256 * cc4 + 128 * (i + 1)],
                               identb[:]),
                           reads=[S1RNG, identb.rng()], writes=[pDrng])
                for cc4 in range(4):
                    b.emit("tensor", lambda e, cc4=cc4, pbf2=pbf2:
                           e.transpose(
                               pbf2[:, 512 + 128 * cc4:512 + 128 * (cc4 + 1)],
                               kvf[:, 128 * cc4:128 * (cc4 + 1)],
                               identb[:]),
                           reads=[kvb.rng(), identb.rng()], writes=[pDrng])
                fo = fout if i == 0 else facc
                forng = fout.rng() if i == 0 else facc.rng()
                b.emit("vector", lambda e, pbf2=pbf2, fo=fo, forng=forng:
                       e.tensor_scalar(
                           fo[:, 0:512], pbf2[:, 0:512], colsC.h[:, 6:7],
                           None, ALU.mult),
                       reads=[pDrng, colsC.rng()], writes=[forng])
                b.emit("vector", lambda e, pbf2=pbf2, fo=fo, forng=forng:
                       e.tensor_scalar(
                           fo[:, 512:1024], pbf2[:, 512:1024],
                           colsC.h[:, 7:8], None, ALU.mult),
                       reads=[pDrng, colsC.rng()], writes=[forng])
                m, r, lrng = ln_stats(c, scol, fo[:], forng,
                                      slot=c.lnslot())
                b.emit("vector", lambda e, m=m, r=r, fo=fo, forng=forng:
                       e.tensor_scalar(fo[:], fo[:], m, r,
                                       ALU.subtract, ALU.mult),
                       reads=[forng, lrng], writes=[forng])
                b.emit("vector", lambda e, s=s, t=t, fo=fo, forng=forng:
                       e.tensor_tensor(
                           stream[:, NT * s + t, :],
                           stream[:, NT * s + t, :], fo[:], ALU.add),
                       reads=[srng, forng], writes=[srng])


EMITTERS["l1attn"] = emit_l1attn



def emit_l1ffn(c, stream, d):
    b, ar, pool, nc = c.b, c.ar, c.pool, c.nc
    identb, onesb = c.identb, c.onesb
    scol = ar.shared("scol", [128, 64], F32)
    xn = ar.shared("xn", [128, 1024], BF16)
    stg = ar.shared("stg", [128, 512], F32)
    fout = ar.shared("fout", [128, 1024], F32)
    fs0 = ar.shared("fs0", [128, 512], F32)
    kvb = ar.shared("kvb", [128, 8, 128], BF16)
    strip0 = ar.shared("strip0", [128, 8, 256], BF16)
    strip1 = ar.shared("strip1", [128, 8, 256], BF16)
    wb0 = ar.shared("wb0", [128, 8, 1024], BF16)
    wb1 = ar.shared("wb1", [128, 8, 1024], BF16)
    wb2 = ar.shared("wb2", [128, 10368], BF16)
    qcols = ar.shared("qcols", [128, 128], F32)
    kvf32 = Buf(kvb.h[:].rearrange("p a b -> p (a b)").bitcast(F32),
                "kvb", kvb.lo, kvb.hi)

    wb2f = wb2.h[:].bitcast(F32)
    xnB = Buf(wb2.h[:, 0:1024], "wb2", wb2.lo, wb2.lo + 2048)
    fs0B = Buf(wb2f[:, 512:1024], "wb2", wb2.lo + 2048, wb2.lo + 4096)
    kvf32B = Buf(wb2f[:, 1024:1536], "wb2", wb2.lo + 4096, wb2.lo + 6144)
    stgB = Buf(wb2f[:, 1536:2048], "wb2", wb2.lo + 6144, wb2.lo + 8192)
    wb0f = wb0.h[:].rearrange("p a b -> p (a b)").bitcast(F32)
    foutA = [Buf(wb0f[:, 0:1024], "wb0", wb0.lo, wb0.lo + 4096),
             Buf(wb0f[:, 1024:2048], "wb0", wb0.lo + 4096, wb0.lo + 8192)]
    st1f32 = strip1.h[:].rearrange("p a b -> p (a b)").bitcast(F32)
    coutB = Buf(st1f32[:, 0:1024], "strip1", strip1.lo, strip1.hi)
    couts = [fout, coutB]
    xnq = [Buf(xn.h, "xn", xn.lo, xn.hi), xnB]
    brf = Buf(wb2.h[0:1, 6144:8192].bitcast(F32),
              "wb2", wb2.lo + 12288, wb2.lo + 16384)
    brb = Buf(wb2.h[0:1, 8192:9216], "wb2", wb2.lo + 16384, wb2.lo + 18432)

    pA, pArng = pool.get(0)
    pB, pBrng = pool.get(1)
    pD, pDrng = pool.get(3)
    pXs = [(pA, pArng), (pB, pBrng)]
    pdbf = pD[:].bitcast(BF16)

    L2 = float(LN2)

    def quantize_tile(p, src, src_rng, dst, dst_rng):
        qb = 64 * p
        qrng = ("sb", qcols.lo + 4 * qb, qcols.lo + 4 * (qb + 64))
        f0, f0r = (fs0[:], fs0.rng()) if p == 0 else (fs0B.h, fs0B.rng())
        kv, kvr = ((kvf32.h, kvf32.rng()) if p == 0
                   else (kvf32B.h, kvf32B.rng()))
        b.emit("vector", lambda e: e.tensor_reduce(
            qcols.h[:, qb + 32:qb + 64],
            src.rearrange("p (a b) -> p a b", b=32),
            AX.X, ALU.max, apply_absolute_value=True),
            reads=[src_rng], writes=[qrng])
        b.emit("vector", lambda e: e.tensor_scalar(
            qcols.h[:, qb + 32:qb + 64], qcols.h[:, qb + 32:qb + 64],
            1.0 / 7.5, 1e-18, ALU.mult, ALU.max),
            reads=[qrng], writes=[qrng])
        b.emit("scalar", lambda e: e.activation(
            qcols.h[:, qb:qb + 32], qcols.h[:, qb + 32:qb + 64], AF.Ln,
            bias=0.0, scale=1.0),
            reads=[qrng], writes=[qrng])
        b.emit("scalar", lambda e: e.activation(
            qcols.h[:, qb:qb + 32], qcols.h[:, qb:qb + 32], AF.Exp,
            bias=0.0, scale=-1.0),
            reads=[qrng], writes=[qrng])
        b.emit("vector", lambda e: e.tensor_tensor(
            src.rearrange("p (a b) -> p a b", b=32),
            src.rearrange("p (a b) -> p a b", b=32),
            qcols.h[:, qb:qb + 32].rearrange("p (a b) -> p a b", b=1)
            .to_broadcast([128, 32, 32]), ALU.mult),
            reads=[src_rng, qrng], writes=[src_rng])
        for q4 in range(4):
            sl = slice(256 * q4, 256 * (q4 + 1))
            yq = src[:, sl]
            b.emit("scalar", lambda e, yq=yq: e.activation(
                f0[:, 0:256], yq, AF.Abs, bias=0.0, scale=1.0),
                reads=[src_rng], writes=[f0r])
            b.emit("vector", lambda e: e.tensor_single_scalar(
                f0[:, 256:512], f0[:, 0:256], 3.875, ALU.is_gt),
                reads=[f0r], writes=[f0r])
            b.emit("vector", lambda e: e.scalar_tensor_tensor(
                kv[:, 0:256], f0[:, 0:256], 1.9375, f0[:, 256:512],
                ALU.is_gt, ALU.add),
                reads=[f0r], writes=[kvr])
            bp = c.cbias(3.0 * L2)
            bn = c.cbias(-3.0 * L2)
            b.emit("scalar", lambda e, bp=bp: e.activation(
                f0[:, 0:256], kv[:, 0:256], AF.Exp, bias=bp,
                scale=-L2),
                reads=[kvr, c.constb.rng()], writes=[f0r])
            b.emit("scalar", lambda e, bn=bn: e.activation(
                kv[:, 256:512], kv[:, 0:256], AF.Exp,
                bias=bn, scale=L2),
                reads=[kvr, c.constb.rng()], writes=[kvr])
            b.emit("vector", lambda e, yq=yq: e.tensor_tensor(
                yq, yq, f0[:, 0:256], ALU.mult),
                reads=[src_rng, f0r], writes=[src_rng])
            b.emit("vector", lambda e, yq=yq: e.tensor_scalar(
                yq, yq, MAGIC, MAGIC, ALU.add, ALU.subtract),
                reads=[src_rng], writes=[src_rng])
            b.emit("vector", lambda e, yq=yq: e.tensor_tensor(
                yq, yq, kv[:, 256:512], ALU.mult),
                reads=[src_rng, kvr], writes=[src_rng])
            b.emit("vector", lambda e, yq=yq, q4=q4: e.tensor_tensor(
                yq.rearrange("p (a b) -> p a b", b=32),
                yq.rearrange("p (a b) -> p a b", b=32),
                qcols.h[:, qb + 32 + 8 * q4:qb + 40 + 8 * q4]
                .rearrange("p (a b) -> p a b", b=1)
                .to_broadcast([128, 8, 32]), ALU.mult),
                reads=[src_rng, qrng], writes=[src_rng])
        b.emit("vector", lambda e: e.tensor_copy(dst[:, 0:1024], src),
               reads=[src_rng], writes=[dst_rng])

    b.emit("sync", lambda e: e.dma_start(
        brf.h, d["L1_ffn_bias"][:].rearrange("(a n) -> a n", a=1)),
        reads=[("dr:L1_ffn_bias", 0, 1)], writes=[brf.rng()], inc=16)
    b.emit("vector", lambda e: e.tensor_copy(brb.h, brf.h),
           reads=[brf.rng()], writes=[brb.rng()])

    def w_load(kc):
        p = kc % 2
        sg = stg if p == 0 else stgB
        fa = foutA[p]
        for hf in range(2):
            b.emit("sync", lambda e, kc=kc, hf=hf, sg=sg: e.dma_start(
                sg[:], d["L1_ffn_weight"][128 * kc:128 * (kc + 1),
                                          512 * hf:512 * (hf + 1)]),
                reads=[("dr:L1_ffn_weight", 0, 1)], writes=[sg.rng()],
                inc=16)
            b.emit("vector", lambda e, hf=hf, sg=sg, fa=fa: e.tensor_copy(
                fa[:, 512 * hf:512 * (hf + 1)], sg[:]),
                reads=[sg.rng()], writes=[fa.rng()])

    def w_quant(kc):
        p = kc % 2
        quantize_tile(p, foutA[p][:], foutA[p].rng(),
                      xnq[p], xnq[p].rng())

    def w_evac(kc):
        p = kc % 2
        prng = ("ps", 3 * 4096 + 2048 * p, 3 * 4096 + 2048 * (p + 1))
        for cc in range(8):
            b.emit("tensor", lambda e, cc=cc, p=p: e.transpose(
                pdbf[:, 1024 * p + 128 * cc:1024 * p + 128 * (cc + 1)],
                xnq[p][:, 128 * cc:128 * (cc + 1)], identb[:]),
                reads=[xnq[p].rng(), identb.rng()], writes=[prng])
        b.emit("vector", lambda e, kc=kc, p=p: e.tensor_copy(
            wb1.h[:, :, 128 * kc:128 * (kc + 1)],
            pdbf[:, 1024 * p:1024 * (p + 1)]
            .rearrange("p (a b) -> p a b", b=128)),
            reads=[prng], writes=[wb1.rng()])
    stagger(8, w_load, w_quant, w_evac)

    NTT = NSEQ * NT

    def t_quant(st):
        s, t = divmod(st, NT)
        p = st % 2
        fa = foutA[p]
        srng = stream.rng(st * 4096, (st + 1) * 4096)
        slot = c.lnslot()
        m, r, nmr, lrng = ln_stats_nmr(c, scol, stream[:, st, :], srng,
                                       slot=slot)
        b.emit("scalar", lambda e, nmr=nmr, r=r, st=st, fa=fa: e.activation(
            fa[:], stream[:, st, :], AF.Identity, bias=nmr, scale=r),
            reads=[srng, lrng], writes=[fa.rng()])
        quantize_tile(p, fa[:], fa.rng(), xnq[p], xnq[p].rng())

    def t_evac(st):
        p = st % 2
        prng = ("ps", 3 * 4096 + 2048 * p, 3 * 4096 + 2048 * (p + 1))
        for cc in range(8):
            b.emit("tensor", lambda e, cc=cc, p=p: e.transpose(
                pdbf[:, 1024 * p + 128 * cc:1024 * p + 128 * (cc + 1)],
                xnq[p][:, 128 * cc:128 * (cc + 1)], identb[:]),
                reads=[xnq[p].rng(), identb.rng()], writes=[prng])
        _, slo, _ = strip0.rng()
        wr = [("sb", slo + 512 * cc + 256 * p, slo + 512 * cc + 256 * (p + 1))
              for cc in range(8)]
        b.emit("vector", lambda e, p=p: e.tensor_copy(
            strip0.h[:, :, 128 * p:128 * (p + 1)],
            pdbf[:, 1024 * p:1024 * (p + 1)]
            .rearrange("p (a b) -> p a b", b=128)),
            reads=[prng], writes=wr)

    def t_mm(st):
        p = st % 2
        pX, pXrng = pXs[p]
        cout = couts[p]
        _, slo, _ = strip0.rng()
        rd = [("sb", slo + 512 * cc + 256 * p,
               slo + 512 * cc + 256 * (p + 1)) for cc in range(8)]
        for hf in range(2):
            for cc in range(8):
                b.emit("tensor", lambda e, hf=hf, cc=cc, p=p, pX=pX:
                       e.matmul(
                           pX[:, 512 * hf:512 * (hf + 1)],
                           strip0.h[:, cc, 128 * p:128 * (p + 1)],
                           wb1.h[:, cc, 512 * hf:512 * (hf + 1)],
                           start=(cc == 0), stop=False),
                       reads=rd + [wb1.rng()], writes=[pXrng])
            b.emit("tensor", lambda e, hf=hf, pX=pX: e.matmul(
                pX[:, 512 * hf:512 * (hf + 1)], onesb[0:1, 0:128],
                brb.h[0:1, 512 * hf:512 * (hf + 1)],
                start=False, stop=True),
                reads=[brb.rng(), onesb.rng()], writes=[pXrng])
        m, r, nmr, lrng = ln_stats_nmr(c, scol, pX[:], pXrng,
                                       slot=c.lnslot())
        b.emit("scalar", lambda e, nmr=nmr, r=r, pX=pX, cout=cout:
               e.activation(cout[:], pX[:], AF.Identity, bias=nmr,
                            scale=r),
               reads=[pXrng, lrng], writes=[cout.rng()])
        srng = stream.rng(st * 4096, (st + 1) * 4096)
        b.emit("vector", lambda e, st=st, cout=cout: e.tensor_tensor(
            stream[:, st, :], stream[:, st, :], cout[:], ALU.add),
            reads=[srng, cout.rng()], writes=[srng])
    stagger(NTT, t_quant, t_evac, t_mm)


EMITTERS["l1ffn"] = emit_l1ffn
